# version 17
# baseline (speedup 1.0000x reference)
import numpy as np
import concourse.bass as bass
import concourse.mybir as mybir
from concourse.bass_utils import run_bass_kernel_spmd

F32 = mybir.dt.float32
BF16 = mybir.dt.bfloat16
AF = mybir.ActivationFunctionType
ALU = mybir.AluOpType
AX = mybir.AxisListType

D, DC, T, S = 1024, 8, 2048, 4096
G = 4
SLAB = 256
QTS, NQT = 512, 4
KB, NKB = 512, 8
FF, JB, NJB = 4096, 512, 8
ZS = 30.0 / 8.0
EPS = 1e-5
ENGINES = ("sync", "tensor", "vector", "scalar", "gpsimd")


class Plan:
    def __init__(self):
        self.q = {e: [] for e in ENGINES}
        self.cnt = {s: 0 for s in ("dL", "dS", "T", "V", "S", "G")}
        self.sems = {}

    def wait(self, eng, sem, val):
        if val is None or val <= 0:
            return
        assert val <= self.cnt[sem], (sem, val, self.cnt[sem])
        self.q[eng].append(lambda e, s=sem, v=val: e.wait_ge(self.sems[s], v))

    def ins(self, eng, fn, sem=None, by=1):
        if sem is None:
            self.q[eng].append(fn)
            return None
        self.cnt[sem] += by
        v = self.cnt[sem]
        self.q[eng].append(lambda e, f=fn, s=sem, b=by: f(e).then_inc(self.sems[s], b))
        return v

    def val(self, sem):
        return self.cnt[sem]


class Pool:
    def __init__(self, plan, banks):
        self.p = plan
        self.st = [dict(ap=b, sem=None, val=0, held=False) for b in banks]
        self.rr = 0

    def acquire(self, eng="tensor"):
        for _ in range(len(self.st)):
            s = self.st[self.rr]
            self.rr = (self.rr + 1) % len(self.st)
            if not s["held"]:
                break
        else:
            raise RuntimeError("psum pool exhausted")
        s["held"] = True
        if s["sem"] is not None:
            self.p.wait(eng, s["sem"], s["val"])
        return s

    def release(self, s, sem, val):
        s["sem"], s["val"], s["held"] = sem, val, False


def build_nc():
    nc = bass.Bass("TRN2", target_bir_lowering=False)

    xo_h = nc.dram_tensor("x_own", [T, D], F32, kind="ExternalInput").ap()
    xt_h = nc.dram_tensor("x_oth", [T, D], F32, kind="ExternalInput").ap()
    w_h = {k: nc.dram_tensor("w" + k, [D, D], F32, kind="ExternalInput").ap()
           for k in ("q", "k", "v", "o")}
    bq_h = nc.dram_tensor("bq", [DC, 128], F32, kind="ExternalInput").ap()
    bk_h = nc.dram_tensor("bk", [DC, 128], F32, kind="ExternalInput").ap()
    bv_h = nc.dram_tensor("bv", [1, D], F32, kind="ExternalInput").ap()
    bo_h = nc.dram_tensor("bo", [1, D], F32, kind="ExternalInput").ap()
    wfc_h = nc.dram_tensor("wfc", [FF, D], F32, kind="ExternalInput").ap()
    wpj_h = nc.dram_tensor("wproj", [D, FF], F32, kind="ExternalInput").ap()
    u_h = nc.dram_tensor("u", [FF, 16], F32, kind="ExternalInput").ap()
    v_h = nc.dram_tensor("v", [16, D], F32, kind="ExternalInput").ap()
    fb_h = nc.dram_tensor("fb", [16, 16], F32, kind="ExternalInput").ap()
    gate_h = nc.dram_tensor("gate", [1, 1], F32, kind="ExternalInput").ap()
    out_h = nc.dram_tensor("out", [T, D], F32, kind="ExternalOutput").ap()
    xspill_h = nc.dram_tensor("xspill", [128, T // 128 * D], F32, kind="Internal").ap()
    fmsp_h = nc.dram_tensor("fmspill", [128, 8 * 8192], BF16, kind="Internal").ap()

    FA = nc.alloc_sbuf_tensor("FA", [128, 27648], F32).ap()
    HA = nc.alloc_sbuf_tensor("HA", [128, 32768], BF16).ap()
    H2 = nc.alloc_sbuf_tensor("H2", [128, 8192], BF16).ap()
    H3 = nc.alloc_sbuf_tensor("H3", [128, 4096], BF16).ap()

    A = lambda o, n: FA[:, o:o + n]
    Bh = lambda o, n: HA[:, o:o + n]
    B2_ = lambda o, n: H2[:, o:o + n]

    f_xres_t = lambda t: A(t * 1024, 1024)
    f_xslab = lambda i: A(16384 + i * 1024, 1024)
    f_xnslab = lambda i: A(18432 + i * 1024, 1024)
    f_V = lambda kt: A(20480 + kt * 256, 256)
    f_attn = lambda i: A(21504 + i * 512, 512)
    f_accatt = lambda pc: A(23552 + pc * 2048, 2048)
    f_accffn = lambda ec: A(ec * 2048, 2048)
    f_w0 = lambda: A(16384, 2048)
    f_w1 = lambda: A(18432, 2048)
    f_ftmp = lambda i: A(20480 + i * 512, 512)
    f_xvf = lambda: FA[0:16, 22528:24576]
    f_xvbf = lambda: FA[0:16, 24576:26624]
    f_xin = lambda i: A(22528 + i * 1024, 1024)
    f_store = lambda i: A(24576 + i * 1024, 1024)
    f_bo = lambda: FA[0:1, 16384:17408]
    f_vsb = lambda: FA[0:16, 20480:21504]
    f_ones128f = lambda: A(21504, 128)

    h_xnhi = lambda i: Bh(i * 1024, 1024)
    h_xnlo = lambda i: Bh(2048 + i * 1024, 1024)
    SLABFM_HI, SLABFM_LO = 4096, 8192
    h_QThi = lambda pc: Bh(12288 + pc * 2048, 2048)
    h_QTlo = lambda pc: Bh(16384 + pc * 2048, 2048)
    h_WT = lambda w, hl: 20480 + (0 if w < 2 else 4096) + hl * 2048
    WOT_BASE = 28672
    h_accThi = lambda pc: Bh(4096 + pc * 2048, 2048)
    h_accTlo = lambda pc: Bh(0 + pc * 2048, 2048)
    X2T_HI, X2T_LO = 16384, 0
    h_vhi = lambda: HA[0:16, 0:1024]
    h_vlo = lambda: HA[0:16, 1024:2048]
    h_ones128 = lambda: Bh(2048, 128)
    h2_KThi = lambda pc: B2_(pc * 512, 512)
    h2_KTlo = lambda pc: B2_(1024 + pc * 512, 512)
    h2_wsphi = lambda: B2_(0, 2048)
    h2_wsplo = lambda: B2_(2048, 2048)
    h2_slabhi = lambda i: B2_(i * 1024, 1024)
    h2_slablo = lambda i: B2_(2048 + i * 1024, 1024)
    h2_a = lambda i: B2_(i * 1024, 1024)
    h2_b = lambda i: B2_(2048 + i * 1024, 1024)
    h2_a5 = lambda i: B2_(i * 512, 512)
    h2_b5 = lambda i: B2_(2048 + i * 512, 512)
    TFC_BASE = 4096
    h2_r2hi = lambda jc: B2_(jc * 512, 512)
    h2_r2lo = lambda jc: B2_(2048 + jc * 512, 512)
    h2_xvhi = lambda: H2[0:16, 0:2048]
    h2_xvlo = lambda: H2[0:16, 2048:4096]
    h3_tpjT = lambda jc, ec: H3[:, jc * 1024 + ec * 128: jc * 1024 + (ec + 1) * 128]

    sm = {}
    for name, shp, dt in (
        ("m_raw", [128, 32], F32), ("mS", [128, 32], F32), ("nmS", [128, 32], F32),
        ("s2", [128, 32], F32), ("msq", [128, 32], F32), ("e1", [128, 32], F32),
        ("veps", [128, 32], F32), ("stdv", [128, 32], F32), ("rr", [128, 32], F32),
        ("nmr", [128, 32], F32),
        ("bq_pp", [128, DC], F32), ("bk_pp", [128, DC], F32), ("bqs", [128, DC], F32),
        ("onescol", [1, 128], F32),
        ("idn", [128, 128], BF16), ("idnf", [128, 128], F32),
        ("gate_s", [1, 1], F32), ("gate_bc", [128, 1], F32),
        ("vT_hi", [128, DC * 16], BF16), ("vT_lo", [128, DC * 16], BF16),
        ("fb_sb", [16, 16], F32), ("fb_hi", [16, 16], BF16), ("fb_lo", [16, 16], BF16),
        ("fbT_hi", [16, 16], BF16), ("fbT_lo", [16, 16], BF16),
        ("u_sb", [128, 64], F32), ("u_hi", [128, 64], BF16), ("u_lo", [128, 64], BF16),
        ("u_f", [128, 64], F32), ("rs_fc", [128, 4], F32), ("sfc2", [128, 32], F32),
        ("uT_hi", [16, JB], BF16), ("uT_lo", [16, JB], BF16),
        ("sfc_m", [128, 32], F32), ("sfc_c", [128, 32], F32),
        ("thp_fc", [128, 32], F32), ("thn_fc", [128, 32], F32),
        ("spj_h0", [128, DC], F32), ("spj_m", [128, DC], F32), ("spj_c", [128, DC], F32),
        ("thp_pj", [128, DC], F32), ("thn_pj", [128, DC], F32),
        ("xvb_hi", [16, 512], BF16), ("xvb_lo", [16, 512], BF16),
        ("bv_g", [1, 256], F32),
    ):
        sm[name] = nc.alloc_sbuf_tensor(name, shp, dt).ap()

    st2 = {k: sm[k][:, 0:16] for k in ("m_raw", "mS", "nmS", "s2", "msq", "e1",
                                       "veps", "stdv", "rr", "nmr")}


    PS = [nc.alloc_psum_tensor(f"ps{i}", [128, 512], F32).ap() for i in range(6)]
    psT_banks = [nc.alloc_psum_tensor(f"psT{i}", [128, 1024], BF16).ap()
                 for i in range(2)]

    p = Plan()
    pool = Pool(p, PS)
    psT_guards = [["V", 0], ["V", 0]]
    psT_rr = [0]
    tb_tv = [0]
    idn, idnf = sm["idn"], sm["idnf"]


    def rows3(src2d, r0, ntiles, width):
        return src2d[r0:r0 + ntiles * 128, :].rearrange("(n p) d -> p n d", p=128)

    def dst3(ap2d, width):
        return ap2d.rearrange("p (n d) -> p n d", d=width)

    def dma(out_ap, in_ap, waits=(), slow=False, sem="dL"):
        for (s, v) in waits:
            p.wait("sync", s, v)
        return p.ins("sync", lambda e, o=out_ap, i=in_ap, sl=slow: e.dma_start(
            out=o, in_=i, allow_slow_non_contiguous=sl), sem, 16)

    def tbatch(srcs, dst, idn_ap):
        bi = psT_rr[0]
        psT_rr[0] = 1 - bi
        psT = psT_banks[bi]
        g_ = psT_guards[bi]
        p.wait("tensor", g_[0], g_[1])
        sp, sf = srcs[0].shape[0], srcs[0].shape[1]
        n = len(srcs)
        tv = None
        for k, src in enumerate(srcs):
            fn = lambda e, s=src, o=k * sp, w=sp, pr=sf, ii=idn_ap, ps_=psT: e.transpose(
                ps_[0:pr, o:o + w], s, ii)
            if k == n - 1:
                tv = p.ins("tensor", fn, "T", 1)
            else:
                p.ins("tensor", fn)
        tb_tv[0] = tv
        p.wait("vector", "T", tv)
        vv = p.ins("vector", lambda e, d=dst, pr=sf, w=n * sp, ps_=psT: e.tensor_copy(
            out=d, in_=ps_[0:pr, 0:w]), "V", 1)
        g_[0], g_[1] = "V", vv
        return vv

    def seg_rhs(base, tt0, ntiles, dc):
        seg = HA[:, base + tt0 * 1024: base + (tt0 + ntiles) * 1024]
        return seg.rearrange("p (i x) -> p i x", x=1024)[:, :, dc * 128:(dc + 1) * 128]

    def wvt_rhs(hl, dc):
        seg = HA[:, h_WT(2, hl): h_WT(2, hl) + 2048]
        return seg.rearrange("p (i x) -> p i x", x=1024)[:, :, dc * 128:(dc + 1) * 128]

    dma(FA[0:8, 21632:21760], bq_h)
    dma(FA[0:8, 21760:21888], bk_h)
    dma(sm["gate_s"], gate_h)
    dma(f_vsb(), v_h)
    dma(sm["fb_sb"], fb_h)
    ld0 = dma(dst3(FA[:, 0:8192], 1024), rows3(xo_h, 0, 8, 1024))
    ldA = ld0

    p.ins("vector", lambda e: e.memset(sm["onescol"], 1.0))
    p.ins("vector", lambda e: e.memset(h_ones128(), 1.0))
    vones = p.ins("vector", lambda e: e.memset(f_ones128f(), 1.0), "V", 1)
    p.wait("gpsimd", "V", vones)
    p.ins("gpsimd", lambda e: e.affine_select(sm["idn"], h_ones128(), [[1, 128]],
                                              ALU.is_equal, 0.0, channel_multiplier=-1))
    gA = p.ins("gpsimd", lambda e: e.affine_select(sm["idnf"], f_ones128f(), [[1, 128]],
                                                   ALU.is_equal, 0.0,
                                                   channel_multiplier=-1), "G", 1)
    p.wait("vector", "dL", ldA)
    p.ins("vector", lambda e: e.tensor_copy(out=h_vhi(), in_=f_vsb()))
    p.ins("vector", lambda e: e.tensor_copy(out=sm["fb_hi"], in_=sm["fb_sb"]))
    p.ins("vector", lambda e: e.drain())
    p.ins("vector", lambda e: e.tensor_tensor(h_vlo(), f_vsb(), h_vhi(), ALU.subtract))
    vA = p.ins("vector", lambda e: e.tensor_sub(sm["fb_lo"], sm["fb_sb"], sm["fb_hi"]),
               "V", 1)
    p.wait("tensor", "V", vA)
    p.wait("tensor", "G", gA)
    st = pool.acquire()
    tv = p.ins("tensor", lambda e, b=st["ap"]: e.matmul(
        b[:, 0:1], sm["onescol"], sm["gate_s"], start=True, stop=True), "T", 1)
    p.wait("vector", "T", tv)
    vv = p.ins("vector", lambda e, b=st["ap"]: e.tensor_copy(out=sm["gate_bc"],
                                                             in_=b[:, 0:1]), "V", 1)
    pool.release(st, "V", vv)

    stb = pool.acquire()
    p.ins("tensor", lambda e, b=stb["ap"]: e.transpose(
        b[0:128, 0:8], FA[0:8, 21632:21760], idnf[0:8, 0:8]))
    tvb = p.ins("tensor", lambda e, b=stb["ap"]: e.transpose(
        b[0:128, 8:16], FA[0:8, 21760:21888], idnf[0:8, 0:8]), "T", 1)
    p.wait("vector", "T", tvb)
    p.ins("vector", lambda e, b=stb["ap"]: e.tensor_copy(out=sm["bq_pp"],
                                                         in_=b[:, 0:8]))
    vvb = p.ins("vector", lambda e, b=stb["ap"]: e.tensor_copy(out=sm["bk_pp"],
                                                               in_=b[:, 8:16]), "V", 1)
    pool.release(stb, "V", vvb)
    p.ins("vector", lambda e: e.drain())
    p.ins("vector", lambda e: e.tensor_scalar(sm["bqs"], sm["bq_pp"], ZS, 0.0,
                                              ALU.mult, ALU.add))

    tbatch([h_vhi()[:, dc * 128:(dc + 1) * 128] for dc in range(DC)],
           sm["vT_hi"], idn[0:16, 0:16])
    tbatch([h_vlo()[:, dc * 128:(dc + 1) * 128] for dc in range(DC)],
           sm["vT_lo"], idn[0:16, 0:16])
    tbatch([sm["fb_hi"]], sm["fbT_hi"], idn[0:16, 0:16])
    tbatch([sm["fb_lo"]], sm["fbT_lo"], idn[0:16, 0:16])

    def stats_pair(x0, x1, col0, m_raw, s2):
        p.wait("scalar", "V", p.val("V"))
        sqv = None
        for i, xs in enumerate((x0, x1)):
            sqv = p.ins("scalar", lambda e, x=xs, i_=i: e.activation(
                out=f_xnslab(i_), in_=x, func=AF.Square, scale=1.0), "S", 1)
        p.wait("vector", "S", sqv)
        for i, xs in enumerate((x0, x1)):
            p.ins("vector", lambda e, x=xs, c=col0 + i, m=m_raw: e.tensor_reduce(
                m[:, c:c + 1], x, AX.X, ALU.add))
        vv = None
        for i in range(2):
            vv = p.ins("vector", lambda e, i_=i, c=col0 + i, s=s2: e.tensor_reduce(
                s[:, c:c + 1], f_xnslab(i_), AX.X, ALU.add), "V", 1)
        return vv

    def statmath(t):
        p.ins("vector", lambda e: e.drain())
        p.ins("vector", lambda e: e.tensor_scalar(t["mS"], t["m_raw"], 1.0 / D, 0.0,
                                                  ALU.mult, ALU.add))
        p.ins("vector", lambda e: e.tensor_scalar(t["e1"], t["s2"], 1.0 / D, EPS,
                                                  ALU.mult, ALU.add))
        p.ins("vector", lambda e: e.drain())
        p.ins("vector", lambda e: e.tensor_mul(t["msq"], t["mS"], t["mS"]))
        p.ins("vector", lambda e: e.tensor_scalar(t["nmS"], t["mS"], -1.0, 0.0,
                                                  ALU.mult, ALU.add))
        p.ins("vector", lambda e: e.drain())
        sv = p.ins("vector", lambda e: e.tensor_sub(t["veps"], t["e1"], t["msq"]), "V", 1)
        p.wait("scalar", "V", sv)
        ssv = p.ins("scalar", lambda e: e.activation(out=t["stdv"], in_=t["veps"],
                                                     func=AF.Sqrt, scale=1.0), "S", 1)
        p.wait("vector", "S", ssv)
        p.ins("vector", lambda e: e.reciprocal(t["rr"], t["stdv"]))
        p.ins("vector", lambda e: e.drain())
        return p.ins("vector", lambda e: e.tensor_mul(t["nmr"], t["nmS"], t["rr"]), "V", 1)

    fm_last_spill = [0]

    def emit_slab512(src_h, row0, cols4, r_, nmr_, t_guard, v_guard=None,
                     spill_idx=None):
        vv = None
        first_ev = True
        t_rnd0 = 0
        sp_guard = fm_last_spill[0]
        for rnd in range(2):
            ld = dma(dst3(FA[:, 16384:18432], 1024),
                     rows3(src_h, row0 + rnd * 256, 2, 1024),
                     waits=tuple(w for w in (("S", p.val("S")),
                                             ("V", v_guard if rnd == 0 else None))
                                 if w[1]))
            p.wait("scalar", "dL", ld)
            p.wait("scalar", "G", p.val("G"))
            sv = None
            for i in range(2):
                c = cols4[rnd * 2 + i]
                sv = p.ins("scalar", lambda e, i_=i, c_=c: e.activation(
                    out=f_xnslab(i_), in_=f_xslab(i_), func=AF.Identity,
                    bias=nmr_[:, c_:c_ + 1], scale=r_[:, c_:c_ + 1]), "S", 1)
            p.ins("scalar", lambda e: e.drain())
            p.wait("scalar", "T", t_guard if rnd == 0 else t_rnd0)
            cv = None
            for i in range(2):
                cv = p.ins("scalar", lambda e, i_=i: e.activation(
                    out=h_xnhi(i_), in_=f_xnslab(i_), func=AF.Identity,
                    scale=1.0), "S", 1)
            p.wait("gpsimd", "S", cv)
            p.wait("gpsimd", "T", t_guard if rnd == 0 else t_rnd0)
            gv = None
            for i in range(2):
                gv = p.ins("gpsimd", lambda e, i_=i: e.tensor_tensor(
                    h_xnlo(i_), f_xnslab(i_), h_xnhi(i_), ALU.subtract), "G", 1)
            p.wait("tensor", "G", gv)
            for tm, fmb in ((h_xnhi, SLABFM_HI), (h_xnlo, SLABFM_LO)):
                for i in range(2):
                    tt = rnd * 2 + i
                    srcs = [tm(i)[:, dc * 128:(dc + 1) * 128] for dc in range(DC)]
                    if first_ev:
                        if t_guard:
                            p.wait("vector", "T", t_guard)
                        p.wait("vector", "dS", sp_guard)
                    first_ev = False
                    vv = tbatch(srcs, HA[:, fmb + tt * 1024: fmb + (tt + 1) * 1024], idn)
            t_rnd0 = tb_tv[0]
        if spill_idx is not None:
            fm_last_spill[0] = dma(
                fmsp_h[:, spill_idx * 8192:(spill_idx + 1) * 8192],
                HA[:, SLABFM_HI:SLABFM_LO + 4096], waits=(("V", vv),), sem="dS")
        return vv

    def reload_slab(idx, t_guard, extra=()):
        return dma(HA[:, SLABFM_HI:SLABFM_LO + 4096],
                   fmsp_h[:, idx * 8192:(idx + 1) * 8192],
                   waits=(("T", t_guard), ("dS", fm_last_spill[0])) + tuple(extra))

    def prep_weight(wk, g, wslot, t_guard, v_guard, stage=16384, dma_waits=None):
        if wk != "o":
            src = rows3(w_h[wk], 256 * g, 2, 1024)
            wdw = 1024
        else:
            src = w_h["o"][:, 256 * g:256 * (g + 1)].rearrange(
                "(c p) d -> p c d", p=128)
            wdw = 256
        wbuf = FA[:, stage:stage + 2048]
        ld = dma(dst3(wbuf, wdw), src,
                 waits=(dma_waits if dma_waits is not None else
                        tuple(w for w in (("T", t_guard), ("S", p.val("S")),
                                          ("V", p.val("V"))) if w[1])))
        p.wait("vector", "dL", ld)
        if v_guard:
            p.wait("vector", "T", v_guard)
        sc = ZS if wk == "q" else 1.0
        p.ins("vector", lambda e, s=sc, b=wbuf: e.tensor_scalar(
            h2_wsphi(), b, s, 0.0, ALU.mult, ALU.add))
        p.ins("vector", lambda e: e.drain())
        vv = p.ins("vector", lambda e, s=sc, b=wbuf: e.scalar_tensor_tensor(
            h2_wsplo(), b, s, h2_wsphi(), ALU.mult, ALU.subtract),
            "V", 1)
        p.wait("tensor", "V", vv)
        for hl, wsp in enumerate((h2_wsphi(), h2_wsplo())):
            if wk != "o":
                for ecl in range(2):
                    srcs = [wsp[:, ecl * 1024 + dc * 128: ecl * 1024 + (dc + 1) * 128]
                            for dc in range(DC)]
                    tbatch(srcs, HA[:, h_WT(wslot, hl) + ecl * 1024:
                                    h_WT(wslot, hl) + (ecl + 1) * 1024], idn)
            else:
                for dcl in range(2):
                    srcs = [wsp[:, ec * 256 + dcl * 128: ec * 256 + (dcl + 1) * 128]
                            for ec in range(DC)]
                    tbatch(srcs, HA[:, WOT_BASE + hl * 2048 + dcl * 1024:
                                    WOT_BASE + hl * 2048 + (dcl + 1) * 1024], idn)

    stv = {k: sm[k][:, 0:16] for k in ("m_raw", "mS", "nmS", "s2", "msq", "e1",
                                       "veps", "stdv", "rr", "nmr")}
    sto = {k: sm[k][:, 16:32] for k in ("m_raw", "mS", "nmS", "s2", "msq", "e1",
                                        "veps", "stdv", "rr", "nmr")}
    sq_own = []
    ld1 = None
    p.wait("scalar", "dL", ld0)
    for j in range(16):
        if j == 8:
            p.wait("scalar", "dL", ld1)
            p.wait("vector", "dL", ld1)
        if j >= 2:
            p.ins("scalar", lambda e: e.drain())
        sq = p.ins("scalar", lambda e, j_=j: e.activation(
            out=f_xnslab(j_ % 2), in_=f_xres_t(j_), func=AF.Square, scale=1.0,
            accum_out=sm["s2"][:, j_:j_ + 1]), "S", 1)
        sq_own.append(sq)
        p.ins("vector", lambda e, j_=j: e.tensor_reduce(
            sm["m_raw"][:, j_:j_ + 1], f_xres_t(j_), AX.X, ALU.add), "V", 1)
        if j == 0:
            ld1 = dma(dst3(FA[:, 8192:16384], 1024), rows3(xo_h, 1024, 8, 1024),
                      waits=(("S", sq),))
        if j == 8:
            prep_weight("q", 0, 0, None, None, stage=23552,
                        dma_waits=(("S", sq),))
    p.wait("vector", "S", sq_own[-1])
    ln1a = statmath(stv)
    p.wait("scalar", "V", ln1a)

    slab_tg = p.val("T")
    kt_tg = None
    v_tg = None
    wt_tg = None
    attn_tg = [None] * 4

    for g in range(G):
        if g > 0:
            prep_weight("q", g, 0, slab_tg, wt_tg)
            slab_tg = p.val("T")
        wprep_vg = p.val("V")

        sqx = None
        for sl in range(4):
            if g == 0:
                vv = emit_slab512(xo_h, sl * 512,
                                  tuple(sl * 4 + i for i in range(4)),
                                  sm["rr"], sm["nmr"], slab_tg,
                                  wprep_vg if sl == 0 else p.val("V"),
                                  spill_idx=sl)
                p.wait("tensor", "V", vv)
            else:
                ld = reload_slab(sl, slab_tg,
                                 ((("V", wprep_vg),) if sl == 0 else ()))
                if sl == 0:
                    p.wait("tensor", "V", wprep_vg)
                p.wait("tensor", "dL", ld)
            banks = []
            tqv = None
            for pc in range(2):
                stq = pool.acquire()
                n_mm = [0]
                for (whl, xhl) in ((0, 0), (1, 0), (0, 1)):
                    for dc in range(DC):
                        n_mm[0] += 1
                        fn = lambda e, b=stq["ap"], w_=whl, x_=xhl, d=dc, pc_=pc, \
                            first=(n_mm[0] == 1), last=(n_mm[0] == 24): e.matmul(
                            b,
                            HA[:, h_WT(0, w_) + pc_ * 1024 + d * 128:
                               h_WT(0, w_) + pc_ * 1024 + (d + 1) * 128],
                            seg_rhs(SLABFM_HI if x_ == 0 else SLABFM_LO, 0, 4, d),
                            start=first, stop=last)
                        if n_mm[0] == 24:
                            tqv = p.ins("tensor", fn, "T", 1)
                        else:
                            p.ins("tensor", fn)
                banks.append(stq)
            p.wait("vector", "T", tqv)
            for pc, stq in enumerate(banks):
                c = 2 * g + pc
                p.ins("vector", lambda e, b=stq["ap"], pc_=pc, c_=c, s=sl:
                      e.tensor_scalar(
                    h_QThi(pc_)[:, s * 512:(s + 1) * 512], b,
                    sm["bqs"][:, c_:c_ + 1], None, ALU.add))
            p.ins("vector", lambda e: e.drain())
            vv2 = None
            for pc, stq in enumerate(banks):
                c = 2 * g + pc
                vv2 = p.ins("vector", lambda e, b=stq["ap"], pc_=pc, c_=c, s=sl:
                            e.scalar_tensor_tensor(
                    h_QTlo(pc_)[:, s * 512:(s + 1) * 512], b,
                    sm["bqs"][:, c_:c_ + 1], h_QThi(pc_)[:, s * 512:(s + 1) * 512],
                    ALU.add, ALU.subtract), "V", 1)
            for stq in banks:
                pool.release(stq, "V", vv2)
            slab_tg = p.val("T")

        dma(sm["bv_g"], bv_h[:, 256 * g:256 * (g + 1)],
            waits=(("T", slab_tg), ("S", p.val("S"))))
        prep_weight("k", g, 0, slab_tg, slab_tg)
        prep_weight("v", g, 2, None, None)
        prep_weight("o", g, 3, None, None)
        slab_tg = p.val("T")
        wprep_vg = p.val("V")

        for kb in range(NKB):
            own = kb < 4
            src = xo_h if own else xt_h
            row0 = (kb if own else kb - 4) * 512
            ccol0 = (kb * 4) if own else (16 + (kb - 4) * 4)
            if g == 0 and kb == 4:
                p.wait("vector", "S", sqx)
                ln1b = statmath(sto)
                p.wait("scalar", "V", ln1b)
            if g == 0 and kb >= 4:
                vv = emit_slab512(src, row0, tuple(ccol0 + i for i in range(4)),
                                  sm["rr"], sm["nmr"], slab_tg,
                                  p.val("V") if kb == 4 else None,
                                  spill_idx=kb)
                p.wait("tensor", "V", vv)
            else:
                ld = reload_slab(kb, slab_tg,
                                 ((("V", wprep_vg),) if kb == 0 else
                                  ((("S", p.val("S")),) if g == 0 else ())))
                if kb == 0:
                    p.wait("tensor", "V", wprep_vg)
                p.wait("tensor", "dL", ld)
            kbanks = []
            tkv = None
            for pc in range(2):
                stk = pool.acquire()
                n_mm = [0]
                for (whl, xhl) in ((0, 0), (1, 0), (0, 1)):
                    for dc in range(DC):
                        n_mm[0] += 1
                        fn = lambda e, b=stk["ap"], w_=whl, x_=xhl, d=dc, pc_=pc, \
                            first=(n_mm[0] == 1), last=(n_mm[0] == 24): e.matmul(
                            b,
                            HA[:, h_WT(0, w_) + pc_ * 1024 + d * 128:
                               h_WT(0, w_) + pc_ * 1024 + (d + 1) * 128],
                            seg_rhs(SLABFM_HI if x_ == 0 else SLABFM_LO, 0, 4, d),
                            start=first, stop=last)
                        if n_mm[0] == 24:
                            tkv = p.ins("tensor", fn, "T", 1)
                        else:
                            p.ins("tensor", fn)
                kbanks.append(stk)
            p.wait("vector", "T", tkv)
            if kt_tg is not None:
                p.wait("vector", "T", kt_tg)
                kt_tg = None
            for pc, stk in enumerate(kbanks):
                c = 2 * g + pc
                p.ins("vector", lambda e, b=stk["ap"], pc_=pc, c_=c: e.tensor_scalar(
                    h2_KThi(pc_), b, sm["bk_pp"][:, c_:c_ + 1], None, ALU.add))
            p.ins("vector", lambda e: e.drain())
            vv2 = None
            for pc, stk in enumerate(kbanks):
                c = 2 * g + pc
                vv2 = p.ins("vector", lambda e, b=stk["ap"], pc_=pc, c_=c:
                            e.scalar_tensor_tensor(
                    h2_KTlo(pc_), b, sm["bk_pp"][:, c_:c_ + 1], h2_KThi(pc_),
                    ALU.add, ALU.subtract), "V", 1)
            for stk in kbanks:
                pool.release(stk, "V", vv2)
            for kt in range(4):
                stv = pool.acquire()
                tvv = None
                n_mm = [0]
                p.ins("tensor", lambda e, b=stv["ap"]: e.matmul(
                    b[:, 0:256], sm["onescol"], sm["bv_g"], start=True, stop=False))
                for (whl, xhl) in ((0, 0), (0, 1), (1, 0)):
                    for dc in range(DC):
                        n_mm[0] += 1
                        base = SLABFM_HI if xhl == 0 else SLABFM_LO
                        fn = lambda e, b=stv["ap"], w_=whl, d=dc, i_=kt, bb=base, \
                            last=(n_mm[0] == 24): e.matmul(
                            b[:, 0:256],
                            HA[:, bb + i_ * 1024 + d * 128:
                               bb + i_ * 1024 + (d + 1) * 128],
                            wvt_rhs(w_, d),
                            start=False, stop=last)
                        if n_mm[0] == 24:
                            tvv = p.ins("tensor", fn, "T", 1)
                        else:
                            p.ins("tensor", fn)
                p.wait("vector", "T", tvv)
                if v_tg is not None:
                    p.wait("vector", "T", v_tg)
                    v_tg = None
                vev = p.ins("vector", lambda e, b=stv["ap"], k=kt: e.tensor_copy(
                    out=f_V(k), in_=b[:, 0:256]), "V", 1)
                pool.release(stv, "V", vev)
            slab_tg = p.val("T")

            xt_jobs = list(range(kb * 4, kb * 4 + 4)) if (g == 0 and kb < 4) else []
            if xt_jobs:
                p.wait("scalar", "G", p.val("G"))

            def emit_xt_tile(j):
                nonlocal sqx
                ldx = dma(f_xslab(j % 2), xt_h[j * 128:(j + 1) * 128, :],
                          waits=(("S", p.val("S")), ("V", p.val("V")),
                                 ("T", slab_tg)))
                p.wait("scalar", "dL", ldx)
                p.ins("scalar", lambda e: e.drain())
                sqx = p.ins("scalar", lambda e, j_=j: e.activation(
                    out=f_xnslab(j_ % 2), in_=f_xslab(j_ % 2),
                    func=AF.Square, scale=1.0,
                    accum_out=sm["s2"][:, 16 + j_:17 + j_]), "S", 1)
                p.wait("vector", "dL", ldx)
                p.ins("vector", lambda e, j_=j: e.tensor_reduce(
                    sm["m_raw"][:, 16 + j_:17 + j_], f_xslab(j_ % 2),
                    AX.X, ALU.add), "V", 1)

            abuf = [0]
            for qt in range(NQT):
                for pc in range(2):
                    if xt_jobs and (qt * 2 + pc) % 2 == 1:
                        emit_xt_tile(xt_jobs.pop(0))
                    pvA = pool.acquire()
                    pvB = pool.acquire()
                    tpv = None
                    pend = []
                    def emit_pv(ent, first, last):
                        nonlocal tpv
                        bufA_, bufB_, sv_, kt_ = ent
                        p.wait("tensor", "S", sv_)
                        p.ins("tensor", lambda e, b=pvA["ap"], k=kt_, pc_=pc, o=bufA_,
                              s_=first, t_=last: e.matmul(
                            b[0:64, :], f_V(k)[:, pc_ * 128: pc_ * 128 + 64], f_attn(o),
                            start=s_, stop=t_, tile_position=(0, 0)))
                        tpv = p.ins("tensor", lambda e, b=pvB["ap"], k=kt_, pc_=pc,
                                    o=bufB_, s_=first, t_=last: e.matmul(
                            b[64:128, :], f_V(k)[:, pc_ * 128 + 64: pc_ * 128 + 128],
                            f_attn(o),
                            start=s_, stop=t_, tile_position=(0, 64)), "T", 1)
                        attn_tg[bufA_] = tpv
                        attn_tg[bufB_] = tpv
                    for kt in range(4):
                        zA = pool.acquire()
                        zB = pool.acquire()
                        tz = None
                        for pi, (khl, qhl) in enumerate(((0, 0), (1, 0), (0, 1))):
                            KT_ = h2_KThi if khl == 0 else h2_KTlo
                            QT_ = h_QThi if qhl == 0 else h_QTlo
                            p.ins("tensor", lambda e, b=zA["ap"], K_=KT_, Q_=QT_, pc_=pc,
                                  k=kt, q=qt, s_=(pi == 0), t_=(pi == 2): e.matmul(
                                b, K_(pc_)[0:64, k * 128:(k + 1) * 128],
                                Q_(pc_)[0:64, q * 512:(q + 1) * 512],
                                start=s_, stop=t_, tile_position=(0, 0)))
                            fnB = lambda e, b=zB["ap"], K_=KT_, Q_=QT_, pc_=pc, \
                                k=kt, q=qt, s_=(pi == 0), t_=(pi == 2): e.matmul(
                                b, K_(pc_)[64:128, k * 128:(k + 1) * 128],
                                Q_(pc_)[64:128, q * 512:(q + 1) * 512],
                                start=s_, stop=t_, tile_position=(64, 0))
                            if pi == 2:
                                tz = p.ins("tensor", fnB, "T", 1)
                            else:
                                p.ins("tensor", fnB)
                        p.wait("scalar", "T", tz)
                        bufA, bufB = abuf[0] % 4, (abuf[0] + 1) % 4
                        abuf[0] += 2
                        for bf in (bufA, bufB):
                            if attn_tg[bf] is not None:
                                p.wait("scalar", "T", attn_tg[bf])
                                attn_tg[bf] = None
                        p.ins("scalar", lambda e, b=zA["ap"], o=bufA: e.activation(
                            out=f_attn(o), in_=b, func=AF.Sin, scale=1.0))
                        sv = p.ins("scalar", lambda e, b=zB["ap"], o=bufB: e.activation(
                            out=f_attn(o), in_=b, func=AF.Sin, scale=1.0), "S", 1)
                        pool.release(zA, "S", sv)
                        pool.release(zB, "S", sv)
                        pend.append((bufA, bufB, sv, kt))
                        if kt >= 1:
                            ent = pend.pop(0)
                            emit_pv(ent, ent[3] == 0, ent[3] == 3)
                    while pend:
                        ent = pend.pop(0)
                        emit_pv(ent, ent[3] == 0, ent[3] == 3)
                    p.wait("vector", "T", tpv)
                    if kb == 0:
                        p.ins("vector", lambda e, b=pvA["ap"], pc_=pc, q=qt:
                              e.tensor_copy(
                            out=f_accatt(pc_)[0:64, q * 512:(q + 1) * 512],
                            in_=b[0:64, :]))
                        vacc = p.ins("vector", lambda e, b=pvB["ap"], pc_=pc, q=qt:
                                     e.tensor_copy(
                            out=f_accatt(pc_)[64:128, q * 512:(q + 1) * 512],
                            in_=b[64:128, :]), "V", 1)
                    else:
                        p.ins("vector", lambda e, b=pvA["ap"], pc_=pc, q=qt:
                              e.tensor_add(
                            f_accatt(pc_)[0:64, q * 512:(q + 1) * 512],
                            f_accatt(pc_)[0:64, q * 512:(q + 1) * 512], b[0:64, :]))
                        vacc = p.ins("vector", lambda e, b=pvB["ap"], pc_=pc, q=qt:
                                     e.tensor_add(
                            f_accatt(pc_)[64:128, q * 512:(q + 1) * 512],
                            f_accatt(pc_)[64:128, q * 512:(q + 1) * 512],
                            b[64:128, :]), "V", 1)
                    pool.release(pvA, "V", vacc)
                    pool.release(pvB, "V", vacc)
            v_tg = p.val("T")
            kt_tg = p.val("T")

        if g == 0:
            bo_ld = dma(f_bo(), bo_h, waits=(("T", slab_tg), ("S", p.val("S"))))
            p.wait("tensor", "dL", bo_ld)
        p.wait("vector", "T", slab_tg)
        p.wait("vector", "dS", fm_last_spill[0])
        p.ins("vector", lambda e: e.drain())
        for pc in range(2):
            p.ins("vector", lambda e, pc_=pc: e.tensor_copy(
                out=h_accThi(pc_), in_=f_accatt(pc_)))
        p.ins("vector", lambda e: e.drain())
        spl = None
        for pc in range(2):
            spl = p.ins("vector", lambda e, pc_=pc: e.tensor_sub(
                h_accTlo(pc_), f_accatt(pc_), h_accThi(pc_)), "V", 1)
        p.wait("tensor", "V", spl)
        if g == G - 1:
            spl3 = spl
            spill_tiles = []
        for tt in range(16):
            for eh in range(2):
                sto = pool.acquire()
                ntot = 6 + (1 if g == 0 else 0)
                n_mm = [0]
                tov = None
                if g == 0:
                    n_mm[0] += 1
                    p.ins("tensor", lambda e, b=sto["ap"], e_=eh: e.matmul(
                        b, sm["onescol"], f_bo()[:, e_ * 512:(e_ + 1) * 512],
                        start=True, stop=False))
                for (ahl, whl) in ((0, 0), (1, 0), (0, 1)):
                    for pc in range(2):
                        n_mm[0] += 1
                        aT = h_accThi if ahl == 0 else h_accTlo
                        fn = lambda e, b=sto["ap"], a_=aT, pc_=pc, w_=whl, t=tt, e_=eh, \
                            first=(n_mm[0] == 1 and g != 0), last=(n_mm[0] == ntot): \
                            e.matmul(
                            b, a_(pc_)[:, t * 128:(t + 1) * 128],
                            HA[:, WOT_BASE + w_ * 2048 + pc_ * 1024 + e_ * 512:
                               WOT_BASE + w_ * 2048 + pc_ * 1024 + (e_ + 1) * 512],
                            start=first, stop=last)
                        if n_mm[0] == ntot:
                            tov = p.ins("tensor", fn, "T", 1)
                        else:
                            p.ins("tensor", fn)
                p.wait("vector", "T", tov)
                vo = p.ins("vector", lambda e, b=sto["ap"], t=tt, e_=eh: e.tensor_add(
                    f_xres_t(t)[:, e_ * 512:(e_ + 1) * 512],
                    f_xres_t(t)[:, e_ * 512:(e_ + 1) * 512], b), "V", 1)
                pool.release(sto, "V", vo)
            if g == G - 1:
                p.wait("sync", "V", vo)
                spill_tiles.append(p.ins("sync", lambda e, t=tt: e.dma_start(
                    out=xspill_h[:, t * 1024:(t + 1) * 1024], in_=f_xres_t(t)),
                    "dS", 16))
            if g == G - 1 and tt % 2 == 1:
                stats_pair(f_xres_t(tt - 1), f_xres_t(tt), tt - 1,
                           st2["m_raw"], st2["s2"])
        wt_tg = p.val("T")
        slab_tg = p.val("T")

    ln2_v = statmath(st2)
    p.wait("scalar", "V", ln2_v)

    ha_free_t = p.val("T")
    c0b_tg = ha_free_t
    for sl in range(8):
        p.wait("scalar", "G", p.val("G"))
        sv = None
        for i in range(2):
            c = sl * 2 + i
            sv = p.ins("scalar", lambda e, i_=i, c_=c: e.activation(
                out=f_xnslab(i_), in_=f_xres_t(c_), func=AF.Identity,
                bias=st2["nmr"][:, c_:c_ + 1], scale=st2["rr"][:, c_:c_ + 1]), "S", 1)
        p.ins("scalar", lambda e: e.drain())
        p.wait("scalar", "T", c0b_tg)
        cv = None
        for i in range(2):
            cv = p.ins("scalar", lambda e, i_=i: e.activation(
                out=h2_slabhi(i_), in_=f_xnslab(i_), func=AF.Identity,
                scale=1.0), "S", 1)
        p.wait("gpsimd", "S", cv)
        p.wait("gpsimd", "T", c0b_tg)
        gv = None
        for i in range(2):
            gv = p.ins("gpsimd", lambda e, i_=i: e.tensor_tensor(
                h2_slablo(i_), f_xnslab(i_), h2_slabhi(i_), ALU.subtract), "G", 1)
        p.wait("tensor", "G", gv)
        first = (sl == 0)
        for tm, base in ((h2_slabhi, X2T_HI), (h2_slablo, X2T_LO)):
            for i in range(2):
                tt = sl * 2 + i
                srcs = [tm(i)[:, dc * 128:(dc + 1) * 128] for dc in range(DC)]
                if first:
                    p.wait("vector", "T", ha_free_t)
                    first = False
                tbatch(srcs, HA[:, base + tt * 1024: base + (tt + 1) * 1024], idn)
        c0b_tg = p.val("T")
    x2t_v = p.val("V")

    spill_v = spill_tiles[-1]
    p.wait("vector", "dS", spill_v)
    p.wait("vector", "G", p.val("G"))
    for ec in range(DC):
        p.ins("vector", lambda e, c=ec: e.memset(f_accffn(c), 0.0),
              "V" if ec == DC - 1 else None, 1)
    accinit_v = p.val("V")

    spj_O = A(20480, 2048)
    spj_L = (A(23552, 2048), A(25600, 2048))
    sq_prev, red_prev = None, None
    for i in range(16):
        ec, hf = i // 2, i % 2
        Lb = spj_L[i % 2]
        if i == 0:
            w = (("V", spl3),)
        else:
            w = (("S", sq_prev),)
        ld = dma(Lb, wpj_h[ec * 128:(ec + 1) * 128, hf * 2048:(hf + 1) * 2048],
                 waits=w)
        p.wait("scalar", "dL", ld)
        if i == 0:
            p.wait("scalar", "T", ha_free_t)
            p.wait("scalar", "V", spl3)
        else:
            p.wait("scalar", "V", red_prev)
        sq_prev = p.ins("scalar", lambda e, b=Lb: e.activation(
            out=spj_O, in_=b, func=AF.Abs, scale=1.0 / FF), "S", 1)
        p.wait("vector", "S", sq_prev)
        dstc = sm["spj_h0"] if hf == 0 else sm["spj_m"]
        red_prev = p.ins("vector", lambda e, c=ec, d=dstc: e.tensor_reduce(
            d[:, c:c + 1], spj_O, AX.X, ALU.add), "V", 1)
    p.ins("vector", lambda e: e.drain())
    p.ins("vector", lambda e: e.tensor_add(sm["spj_m"], sm["spj_m"], sm["spj_h0"]))
    p.ins("vector", lambda e: e.drain())
    p.ins("vector", lambda e: e.drain())
    p.ins("vector", lambda e: e.tensor_scalar(sm["spj_c"], sm["spj_m"], 1e-5, 0.0,
                                              ALU.max, ALU.add))
    p.ins("vector", lambda e: e.drain())
    p.ins("vector", lambda e: e.tensor_scalar(sm["thp_pj"], sm["spj_c"], 0.5, 0.0,
                                              ALU.mult, ALU.add))
    spj_v = p.ins("vector", lambda e: e.tensor_scalar(sm["thn_pj"], sm["spj_c"], -0.5,
                                                      0.0, ALU.mult, ALU.add), "V", 1)

    p.wait("tensor", "V", x2t_v)
    for tq in range(NQT):
        stx = pool.acquire()
        n_mm = [0]
        txv = None
        for (vhl, xhl) in ((0, 0), (1, 0), (0, 1)):
            vT = sm["vT_hi"] if vhl == 0 else sm["vT_lo"]
            for dc in range(DC):
                n_mm[0] += 1
                fn = lambda e, b=stx["ap"], v_=vT, d=dc, x_=xhl, q=tq, \
                    first=(n_mm[0] == 1), last=(n_mm[0] == 24): e.matmul(
                    b[0:16, :], v_[:, d * 16:(d + 1) * 16],
                    seg_rhs(X2T_HI if x_ == 0 else X2T_LO, q * 4, 4, d),
                    start=first, stop=last)
                if n_mm[0] == 24:
                    txv = p.ins("tensor", fn, "T", 1)
                else:
                    p.ins("tensor", fn)
        p.wait("vector", "T", txv)
        vv = p.ins("vector", lambda e, b=stx["ap"], q=tq: e.tensor_copy(
            out=f_xvf()[:, q * 512:(q + 1) * 512], in_=b[0:16, :]), "V", 1)
        pool.release(stx, "V", vv)
    p.wait("gpsimd", "V", vv)
    hv = p.ins("gpsimd", lambda e: e.tensor_copy(out=h2_xvhi(), in_=f_xvf()), "G", 1)
    p.wait("gpsimd", "G", hv)
    gv = p.ins("gpsimd", lambda e: e.tensor_tensor(
        h2_xvlo(), f_xvf(), h2_xvhi(), ALU.subtract), "G", 1)
    p.wait("tensor", "G", gv)
    for tq in range(NQT):
        stx = pool.acquire()
        txb = None
        for pi, (fhl, xhl) in enumerate(((0, 0), (1, 0), (0, 1))):
            fbT = sm["fbT_hi"] if fhl == 0 else sm["fbT_lo"]
            xv = h2_xvhi() if xhl == 0 else h2_xvlo()
            fn = lambda e, b=stx["ap"], f_=fbT, x_=xv, q=tq, first=(pi == 0), \
                last=(pi == 2): e.matmul(
                b[0:16, :], f_, x_[:, q * 512:(q + 1) * 512], start=first, stop=last)
            if pi == 2:
                txb = p.ins("tensor", fn, "T", 1)
            else:
                p.ins("tensor", fn)
        p.wait("vector", "T", txb)
        vv = p.ins("vector", lambda e, b=stx["ap"], q=tq: e.tensor_scalar(
            f_xvbf()[:, q * 512:(q + 1) * 512], b[0:16, :],
            sm["gate_bc"][0:16, :], None, ALU.mult), "V", 1)
        pool.release(stx, "V", vv)
    xvb_v = p.val("V")

    r2_tg = p.val("T")
    for jb in range(NJB):
        for hb in range(2):
            ld = dma(dst3(f_w0(), 1024),
                     rows3(wfc_h, jb * 512 + hb * 256, 2, 1024),
                     waits=(("S", p.val("S")), ("V", p.val("V"))))
            p.wait("scalar", "dL", ld)
            sq = p.ins("scalar", lambda e: e.activation(
                out=f_w1(), in_=f_w0(), func=AF.Abs, scale=1.0 / D), "S", 1)
            p.wait("vector", "S", sq)
            for j in range(2):
                c = jb * 4 + hb * 2 + j
                p.ins("vector", lambda e, c_=c, j_=j: e.tensor_reduce(
                    sm["sfc_m"][:, c_:c_ + 1],
                    f_w1()[:, j_ * 1024:(j_ + 1) * 1024],
                    AX.X, ALU.add), "V", 1)
            c0 = jb * 4 + hb * 2
            p.ins("vector", lambda e: e.drain())
            p.ins("vector", lambda e, c_=c0: e.tensor_scalar(
                sm["sfc_c"][:, c_:c_ + 2], sm["sfc_m"][:, c_:c_ + 2], 1e-5, 0.0,
                ALU.max, ALU.add))
            p.ins("vector", lambda e: e.drain())
            p.ins("vector", lambda e, c_=c0: e.tensor_scalar(
                sm["thp_fc"][:, c_:c_ + 2], sm["sfc_c"][:, c_:c_ + 2], 0.5, 0.0,
                ALU.mult, ALU.add))
            p.ins("vector", lambda e, c_=c0: e.tensor_scalar(
                sm["thn_fc"][:, c_:c_ + 2], sm["sfc_c"][:, c_:c_ + 2], -0.5, 0.0,
                ALU.mult, ALU.add))
            p.ins("vector", lambda e: e.drain())
            if r2_tg is not None:
                p.wait("vector", "T", r2_tg)
                r2_tg = None
            for j in range(2):
                c = jb * 4 + hb * 2 + j
                p.ins("vector", lambda e, j_=j, c_=c: e.tensor_scalar(
                    h2_a(j_), f_w0()[:, j_ * 1024:(j_ + 1) * 1024],
                    sm["thp_fc"][:, c_:c_ + 1], None, ALU.is_gt))
                p.ins("vector", lambda e, j_=j, c_=c: e.tensor_scalar(
                    h2_b(j_), f_w0()[:, j_ * 1024:(j_ + 1) * 1024],
                    sm["thn_fc"][:, c_:c_ + 1], None, ALU.is_lt))
            p.ins("vector", lambda e: e.drain())
            tern_v = None
            for j in range(2):
                tern_v = p.ins("vector", lambda e, j_=j: e.tensor_tensor(
                    h2_a(j_), h2_a(j_), h2_b(j_), ALU.subtract), "V", 1)
            p.wait("tensor", "V", tern_v)
            for j in range(2):
                jt = hb * 2 + j
                srcs = [h2_a(j)[:, dc * 128:(dc + 1) * 128] for dc in range(DC)]
                tbatch(srcs, H2[:, TFC_BASE + jt * 1024: TFC_BASE + (jt + 1) * 1024], idn)
        tfc_t = p.val("T")

        for hf in range(2):
            ld = dma(dst3(f_w0(), 512),
                     wpj_h[hf * 512:(hf + 1) * 512, jb * 512:(jb + 1) * 512]
                     .rearrange("(c p) d -> p c d", p=128),
                     waits=(("V", p.val("V")), ("T", tfc_t if hf == 0 else 0)))
            p.wait("vector", "dL", ld)
            for el in range(4):
                ec = hf * 4 + el
                p.ins("vector", lambda e, l=el, c=ec: e.tensor_scalar(
                    h2_a5(l), f_w0()[:, l * 512:(l + 1) * 512],
                    sm["thp_pj"][:, c:c + 1], None, ALU.is_gt))
                p.ins("vector", lambda e, l=el, c=ec: e.tensor_scalar(
                    h2_b5(l), f_w0()[:, l * 512:(l + 1) * 512],
                    sm["thn_pj"][:, c:c + 1], None, ALU.is_lt))
            p.ins("vector", lambda e: e.drain())
            ej_v = None
            for el in range(4):
                ej_v = p.ins("vector", lambda e, l=el: e.tensor_tensor(
                    h2_a5(l), h2_a5(l), h2_b5(l), ALU.subtract), "V", 1)
            p.wait("tensor", "V", ej_v)
            for jc in range(4):
                srcs = [h2_a5(el)[:, jc * 128:(jc + 1) * 128] for el in range(4)]
                tbatch(srcs, H3[:, jc * 1024 + hf * 512: jc * 1024 + (hf + 1) * 512], idn)
        tpj_t = p.val("T")
        r2_tg = tpj_t

        ld = dma(dst3(sm["u_sb"], 16),
                 u_h[jb * 512:(jb + 1) * 512, :].rearrange("(c p) d -> p c d", p=128),
                 waits=(("T", p.val("T")),))
        c0 = jb * 4
        p.ins("vector", lambda e: e.drain())
        p.ins("vector", lambda e, c_=c0: e.reciprocal(
            sm["rs_fc"], sm["sfc_c"][:, c_:c_ + 4]))
        p.ins("vector", lambda e, c_=c0: e.tensor_mul(
            sm["sfc2"][:, c_:c_ + 4], sm["sfc_c"][:, c_:c_ + 4],
            sm["sfc_c"][:, c_:c_ + 4]))
        p.wait("vector", "dL", ld)
        p.ins("vector", lambda e: e.drain())
        for jt in range(4):
            p.ins("vector", lambda e, j_=jt: e.tensor_scalar(
                sm["u_f"][:, j_ * 16:(j_ + 1) * 16],
                sm["u_sb"][:, j_ * 16:(j_ + 1) * 16],
                sm["rs_fc"][:, j_:j_ + 1], None, ALU.mult))
        p.ins("vector", lambda e: e.drain())
        p.ins("vector", lambda e: e.tensor_copy(out=sm["u_hi"], in_=sm["u_f"]))
        p.ins("vector", lambda e: e.drain())
        uv = p.ins("vector", lambda e: e.tensor_sub(sm["u_lo"], sm["u_f"], sm["u_hi"]),
                   "V", 1)
        p.wait("tensor", "V", uv)
        tbatch([sm["u_hi"][:, jt * 16:(jt + 1) * 16] for jt in range(4)], sm["uT_hi"], idn)
        tbatch([sm["u_lo"][:, jt * 16:(jt + 1) * 16] for jt in range(4)], sm["uT_lo"], idn)
        p.wait("tensor", "V", p.val("V"))

        def emit_proj(tq_, hi_v, lo_vv):
            p.wait("tensor", "S", hi_v)
            p.wait("tensor", "V", lo_vv)
            for ec in range(DC):
                stp = pool.acquire()
                n_mm = [0]
                tp = None
                for rhl in (0, 1):
                    r2 = h2_r2hi if rhl == 0 else h2_r2lo
                    for jc in range(4):
                        n_mm[0] += 1
                        fn = lambda e, b=stp["ap"], c=ec, j=jc, r_=r2, \
                            first=(n_mm[0] == 1), last=(n_mm[0] == 8): e.matmul(
                            b, H3[:, j * 1024 + c * 128: j * 1024 + (c + 1) * 128],
                            r_(j), start=first, stop=last)
                        if n_mm[0] == 8:
                            tp = p.ins("tensor", fn, "T", 1)
                        else:
                            p.ins("tensor", fn)
                p.wait("vector", "T", tp)
                va = p.ins("vector", lambda e, b=stp["ap"], c=ec, q=tq_:
                           e.scalar_tensor_tensor(
                    f_accffn(c)[:, q * 512:(q + 1) * 512], b,
                    sm["spj_c"][:, c:c + 1],
                    f_accffn(c)[:, q * 512:(q + 1) * 512], ALU.mult, ALU.add), "V", 1)
                pool.release(stp, "V", va)

        hi_s = lo_v = None
        r2g = r2_tg
        lrT = r2_tg
        for tq in range(NQT):
            p.wait("gpsimd", "V", xvb_v)
            p.wait("gpsimd", "T", lrT)
            hv = p.ins("gpsimd", lambda e, q=tq: e.tensor_copy(
                out=sm["xvb_hi"], in_=f_xvbf()[:, q * 512:(q + 1) * 512]), "G", 1)
            p.wait("gpsimd", "G", hv)
            gxv = p.ins("gpsimd", lambda e, q=tq: e.tensor_tensor(
                sm["xvb_lo"], f_xvbf()[:, q * 512:(q + 1) * 512], sm["xvb_hi"],
                ALU.subtract), "G", 1)
            p.wait("tensor", "G", gxv)
            fcb = []
            for jc in range(4):
                stf = pool.acquire()
                n_mm = [0]
                tf = None
                for xhl in (0, 1):
                    for dc in range(DC):
                        n_mm[0] += 1
                        fn = lambda e, b=stf["ap"], j=jc, d=dc, x_=xhl, q=tq, \
                            first=(n_mm[0] == 1): e.matmul(
                            b, H2[:, TFC_BASE + j * 1024 + d * 128:
                                  TFC_BASE + j * 1024 + (d + 1) * 128],
                            seg_rhs(X2T_HI if x_ == 0 else X2T_LO, q * 4, 4, d),
                            start=first, stop=False)
                        p.ins("tensor", fn)
                for pi, (uhl, xhl) in enumerate(((0, 0), (1, 0), (0, 1))):
                    uT = sm["uT_hi"] if uhl == 0 else sm["uT_lo"]
                    xb = sm["xvb_hi"] if xhl == 0 else sm["xvb_lo"]
                    fn = lambda e, b=stf["ap"], u_=uT, x_=xb, j=jc, \
                        last=(pi == 2): e.matmul(
                        b, u_[:, j * 128:(j + 1) * 128], x_,
                        start=False, stop=last)
                    if pi == 2:
                        tf = p.ins("tensor", fn, "T", 1)
                    else:
                        p.ins("tensor", fn)
                fcb.append((jc, stf, tf))
            lrT = fcb[-1][2]
            p.wait("vector", "T", fcb[-1][2])
            if hi_s:
                p.wait("vector", "S", hi_s)
            rbv = None
            for pr in (0, 1):
                if pr == 1:
                    p.ins("vector", lambda e: e.drain())
                for (jc, stf_, tf_) in fcb[pr * 2:pr * 2 + 2]:
                    c = jb * 4 + jc
                    p.ins("vector", lambda e, b=stf_["ap"], j_=jc, c_=c:
                          e.tensor_scalar(
                        A(26624 + (j_ % 2) * 512, 512), b,
                        sm["sfc2"][:, c_:c_ + 1], None, ALU.mult))
                p.ins("vector", lambda e: e.drain())
                for (jc, stf_, tf_) in fcb[pr * 2:pr * 2 + 2]:
                    rbv = p.ins("vector", lambda e, b=stf_["ap"], j_=jc:
                                e.scalar_tensor_tensor(
                        f_ftmp(j_), A(26624 + (j_ % 2) * 512, 512), 0.0, b,
                        ALU.max, ALU.mult), "V", 1)
            for (jc, stf_, tf_) in fcb:
                pool.release(stf_, "V", rbv)
            if tq >= 1:
                emit_proj(tq - 1, hi_s, lo_v)
                r2g = p.val("T")
            p.wait("scalar", "V", rbv)
            p.wait("scalar", "T", r2g)
            nhi = None
            for (jc, stf_, tf_) in fcb:
                nhi = p.ins("scalar", lambda e, j_=jc: e.activation(
                    out=h2_r2hi(j_), in_=f_ftmp(j_), func=AF.Identity,
                    scale=1.0), "S", 1)
            hi_s = nhi
            p.wait("vector", "S", hi_s)
            p.wait("vector", "T", r2g)
            p.ins("vector", lambda e: e.drain())
            for (jc, stf_, tf_) in fcb:
                lo_v = p.ins("vector", lambda e, j_=jc: e.tensor_tensor(
                    h2_r2lo(j_), f_ftmp(j_), h2_r2hi(j_), ALU.subtract), "V", 1)
        emit_proj(NQT - 1, hi_s, lo_v)
        r2_tg = p.val("T")

    ffn_v = p.val("V")
    p.wait("tensor", "V", ffn_v)
    p.wait("vector", "S", p.val("S"))
    p.wait("vector", "G", p.val("G"))
    p.wait("sync", "G", p.val("G"))
    store_vals = []
    xin_vals = []
    for tt in range(16):
        sbuf = tt % 2
        xw = [("dS", spill_v if tt == 0 else store_vals[tt - 1])]
        if tt >= 2:
            xw.append(("V", xin_vals[tt - 2]))
        xld = dma(f_xin(sbuf), xspill_h[:, tt * 1024:(tt + 1) * 1024], waits=tuple(xw))
        vv = None
        for eh in range(2):
            stx = pool.acquire()
            ti = None
            for k in range(4):
                ec = eh * 4 + k
                fn = lambda e, b=stx["ap"], t=tt, c=ec, o=k * 128: e.transpose(
                    b[:, o:o + 128], f_accffn(c)[:, t * 128:(t + 1) * 128], idnf)
                if k == 3:
                    ti = p.ins("tensor", fn, "T", 1)
                else:
                    p.ins("tensor", fn)
            p.wait("vector", "T", ti)
            p.wait("vector", "dL", xld)
            if tt >= 2 and eh == 0:
                p.wait("vector", "dS", store_vals[tt - 1])
            for k in range(4):
                ec = eh * 4 + k
                vv = p.ins("vector", lambda e, b=stx["ap"], s=sbuf, c=ec, o=k * 128:
                           e.tensor_add(
                    f_store(s)[:, c * 128:(c + 1) * 128], b[:, o:o + 128],
                    f_xin(s)[:, c * 128:(c + 1) * 128]),
                    "V", 1)
            pool.release(stx, "V", vv)
        xin_vals.append(vv)
        p.wait("sync", "V", vv)
        dv = p.ins("sync", lambda e, t=tt, s=sbuf: e.dma_start(
            out=out_h[t * 128:(t + 1) * 128, :], in_=f_store(s)), "dS", 16)
        store_vals.append(dv)
    p.wait("sync", "dS", p.val("dS"))

    with (
        nc.semaphore("s_dL") as s_dL,
        nc.semaphore("s_dS") as s_dS,
        nc.semaphore("s_T") as s_T,
        nc.semaphore("s_V") as s_V,
        nc.semaphore("s_S") as s_S,
        nc.semaphore("s_G") as s_G,
        nc.Block() as block,
    ):
        p.sems = {"dL": s_dL, "dS": s_dS, "T": s_T, "V": s_V, "S": s_S, "G": s_G}

        @block.sync
        def _(sync):
            for fn in p.q["sync"]:
                fn(sync)

        @block.tensor
        def _(tensor):
            for fn in p.q["tensor"]:
                fn(tensor)

        @block.vector
        def _(vector):
            for fn in p.q["vector"]:
                fn(vector)

        @block.scalar
        def _(scalar):
            for fn in p.q["scalar"]:
                fn(scalar)

        @block.gpsimd
        def _(gpsimd):
            for fn in p.q["gpsimd"]:
                fn(gpsimd)

    return nc


_NC_CACHE = [None]


def kernel(L0_attn_x, L0_attn_Wq, L0_attn_bq, L0_attn_Wk, L0_attn_bk, L0_attn_Wv,
           L0_attn_bv, L0_attn_Wo, L0_attn_bo, L0_ffn_fast_b, L0_ffn_w_fc,
           L0_ffn_w_proj, L0_ffn_u, L0_ffn_v, L0_ffn_gate):
    x = np.ascontiguousarray(np.asarray(L0_attn_x, dtype=np.float32))
    Bn, Sn, Dn = x.shape
    assert (Bn, Sn, Dn) == (4, 4096, 1024)
    if _NC_CACHE[0] is None:
        _NC_CACHE[0] = build_nc()
    nc = _NC_CACHE[0]

    f32 = lambda a: np.ascontiguousarray(np.asarray(a, dtype=np.float32))
    com = {
        "wq": f32(L0_attn_Wq), "wk": f32(L0_attn_Wk), "wv": f32(L0_attn_Wv),
        "wo": f32(L0_attn_Wo),
        "bq": f32(L0_attn_bq).reshape(DC, 128), "bk": f32(L0_attn_bk).reshape(DC, 128),
        "bv": f32(L0_attn_bv).reshape(1, D), "bo": f32(L0_attn_bo).reshape(1, D),
        "wfc": f32(L0_ffn_w_fc), "wproj": f32(L0_ffn_w_proj),
        "u": f32(L0_ffn_u), "v": f32(L0_ffn_v), "fb": f32(L0_ffn_fast_b),
        "gate": f32(L0_ffn_gate).reshape(1, 1),
    }
    in_maps = []
    for c in range(8):
        b, h = c // 2, c % 2
        m = dict(com)
        m["x_own"] = np.ascontiguousarray(x[b, h * T:(h + 1) * T, :])
        m["x_oth"] = np.ascontiguousarray(x[b, (1 - h) * T:(2 - h) * T, :])
        in_maps.append(m)
    res = run_bass_kernel_spmd(nc, in_maps, core_ids=list(range(8)))
    out = np.empty((Bn, Sn, Dn), dtype=np.float32)
    for c in range(8):
        b, h = c // 2, c % 2
        out[b, h * T:(h + 1) * T, :] = np.asarray(res.results[c]["out"])
    return out


# version 20
# speedup vs baseline: 1.0421x; 1.0421x over previous
import numpy as np
import concourse.bass as bass
import concourse.mybir as mybir
from concourse.bass_utils import run_bass_kernel_spmd

F32 = mybir.dt.float32
BF16 = mybir.dt.bfloat16
AF = mybir.ActivationFunctionType
ALU = mybir.AluOpType
AX = mybir.AxisListType

D, DC, T, S = 1024, 8, 2048, 4096
G = 4
SLAB = 256
QTS, NQT = 512, 4
KB, NKB = 512, 8
FF, JB, NJB = 4096, 512, 8
ZS = 30.0 / 8.0
EPS = 1e-5
ENGINES = ("sync", "tensor", "vector", "scalar", "gpsimd")


class Plan:
    def __init__(self):
        self.q = {e: [] for e in ENGINES}
        self.cnt = {s: 0 for s in ("dL", "dS", "dG", "T", "V", "S", "G")}
        self.sems = {}

    def wait(self, eng, sem, val):
        if val is None or val <= 0:
            return
        assert val <= self.cnt[sem], (sem, val, self.cnt[sem])
        self.q[eng].append(lambda e, s=sem, v=val: e.wait_ge(self.sems[s], v))

    def ins(self, eng, fn, sem=None, by=1):
        if sem is None:
            self.q[eng].append(fn)
            return None
        self.cnt[sem] += by
        v = self.cnt[sem]
        self.q[eng].append(lambda e, f=fn, s=sem, b=by: f(e).then_inc(self.sems[s], b))
        return v

    def val(self, sem):
        return self.cnt[sem]


class Pool:
    def __init__(self, plan, banks):
        self.p = plan
        self.st = [dict(ap=b, sem=None, val=0, held=False) for b in banks]
        self.rr = 0

    def acquire(self, eng="tensor"):
        for _ in range(len(self.st)):
            s = self.st[self.rr]
            self.rr = (self.rr + 1) % len(self.st)
            if not s["held"]:
                break
        else:
            raise RuntimeError("psum pool exhausted")
        s["held"] = True
        if s["sem"] is not None:
            self.p.wait(eng, s["sem"], s["val"])
        return s

    def release(self, s, sem, val):
        s["sem"], s["val"], s["held"] = sem, val, False


def build_nc():
    nc = bass.Bass("TRN2", target_bir_lowering=False)

    xo_h = nc.dram_tensor("x_own", [T, D], F32, kind="ExternalInput").ap()
    xt_h = nc.dram_tensor("x_oth", [T, D], F32, kind="ExternalInput").ap()
    w_h = {k: nc.dram_tensor("w" + k, [D, D], F32, kind="ExternalInput").ap()
           for k in ("q", "k", "v", "o")}
    bq_h = nc.dram_tensor("bq", [DC, 128], F32, kind="ExternalInput").ap()
    bk_h = nc.dram_tensor("bk", [DC, 128], F32, kind="ExternalInput").ap()
    bv_h = nc.dram_tensor("bv", [1, D], F32, kind="ExternalInput").ap()
    bo_h = nc.dram_tensor("bo", [1, D], F32, kind="ExternalInput").ap()
    wfc_h = nc.dram_tensor("wfc", [FF, D], F32, kind="ExternalInput").ap()
    wpj_h = nc.dram_tensor("wproj", [D, FF], F32, kind="ExternalInput").ap()
    u_h = nc.dram_tensor("u", [FF, 16], F32, kind="ExternalInput").ap()
    v_h = nc.dram_tensor("v", [16, D], F32, kind="ExternalInput").ap()
    fb_h = nc.dram_tensor("fb", [16, 16], F32, kind="ExternalInput").ap()
    gate_h = nc.dram_tensor("gate", [1, 1], F32, kind="ExternalInput").ap()
    out_h = nc.dram_tensor("out", [T, D], F32, kind="ExternalOutput").ap()
    xvb_f = None
    xspill_h = nc.dram_tensor("xspill", [128, T // 128 * D], F32, kind="Internal").ap()
    fmsp_h = nc.dram_tensor("fmspill", [128, 8 * 8192], BF16, kind="Internal").ap()
    xvb_h = nc.dram_tensor("xvbspill", [16, 2048], F32, kind="Internal").ap()

    FA = nc.alloc_sbuf_tensor("FA", [128, 27648], F32).ap()
    HA = nc.alloc_sbuf_tensor("HA", [128, 32768], BF16).ap()
    H2 = nc.alloc_sbuf_tensor("H2", [128, 8192], BF16).ap()
    H3 = nc.alloc_sbuf_tensor("H3", [128, 4096], BF16).ap()

    A = lambda o, n: FA[:, o:o + n]
    xvb_f = FA[0:16, 22528:23040]
    Bh = lambda o, n: HA[:, o:o + n]
    B2_ = lambda o, n: H2[:, o:o + n]

    f_xres_t = lambda t: A(t * 1024, 1024)
    f_xslab = lambda i: A(16384 + i * 1024, 1024)
    f_xnslab = lambda i: A(18432 + i * 1024, 1024)
    f_V = lambda kt: A(20480 + kt * 256, 256)
    f_attn = lambda i: A(21504 + i * 512, 512)
    f_accatt = lambda pc: A(23552 + pc * 2048, 2048)
    f_accffn = lambda ec: A(ec * 2048, 2048)
    f_w0 = lambda: A(16384, 2048)
    f_w1 = lambda: A(18432, 2048)
    f_ftmp = lambda i: A(20480 + i * 512, 512)
    f_xin = lambda i: A(22528 + i * 1024, 1024)
    f_store = lambda i: A(24576 + i * 1024, 1024)
    f_bo = lambda: FA[0:1, 16384:17408]
    f_vsb = lambda: FA[0:16, 20480:21504]
    f_ones128f = lambda: A(21504, 128)

    h_xnhi = lambda i: Bh(i * 1024, 1024)
    h_xnlo = lambda i: Bh(2048 + i * 1024, 1024)
    SLABFM_HI, SLABFM_LO = 4096, 8192
    h_QThi = lambda pc: Bh(12288 + pc * 2048, 2048)
    h_QTlo = lambda pc: Bh(16384 + pc * 2048, 2048)
    h_WT = lambda w, hl: 20480 + (0 if w < 2 else 4096) + hl * 2048
    WOT_BASE = 28672
    h_accThi = lambda pc: Bh(4096 + pc * 2048, 2048)
    h_accTlo = lambda pc: Bh(0 + pc * 2048, 2048)
    X2T_HI, X2T_LO = 16384, 0
    h_vhi = lambda: HA[0:16, 0:1024]
    h_vlo = lambda: HA[0:16, 1024:2048]
    h_ones128 = lambda: Bh(2048, 128)
    h2_KThi = lambda pc: B2_(pc * 512, 512)
    h2_KTlo = lambda pc: B2_(1024 + pc * 512, 512)
    h2_wsphi = lambda: B2_(0, 2048)
    h2_wsplo = lambda: B2_(2048, 2048)
    h2_slabhi = lambda i: B2_(i * 1024, 1024)
    h2_slablo = lambda i: B2_(2048 + i * 1024, 1024)
    h2_a = lambda i: B2_(i * 1024, 1024)
    h2_b = lambda i: B2_(2048 + i * 1024, 1024)
    h2_a5 = lambda i: B2_(i * 512, 512)
    h2_b5 = lambda i: B2_(2048 + i * 512, 512)
    TFC_BASE = 4096
    h2_r2hi = lambda jc: B2_(jc * 512, 512)
    h2_r2lo = lambda jc: B2_(2048 + jc * 512, 512)
    h2_xvhi = lambda: H2[0:16, 0:2048]
    h2_xvlo = lambda: H2[0:16, 2048:4096]
    h3_tpjT = lambda jc, ec: H3[:, jc * 1024 + ec * 128: jc * 1024 + (ec + 1) * 128]

    sm = {}
    for name, shp, dt in (
        ("m_raw", [128, 32], F32), ("mS", [128, 32], F32), ("nmS", [128, 32], F32),
        ("s2", [128, 32], F32), ("msq", [128, 32], F32), ("e1", [128, 32], F32),
        ("veps", [128, 32], F32), ("stdv", [128, 32], F32), ("rr", [128, 32], F32),
        ("nmr", [128, 32], F32),
        ("bq_pp", [128, DC], F32), ("bk_pp", [128, DC], F32), ("bqs", [128, DC], F32),
        ("onescol", [1, 128], F32),
        ("idn", [128, 128], BF16), ("idnf", [128, 128], F32),
        ("gate_s", [1, 1], F32), ("gate_bc", [128, 1], F32),
        ("vT_hi", [128, DC * 16], BF16), ("vT_lo", [128, DC * 16], BF16),
        ("fb_sb", [16, 16], F32), ("fb_hi", [16, 16], BF16), ("fb_lo", [16, 16], BF16),
        ("fbT_hi", [16, 16], BF16), ("fbT_lo", [16, 16], BF16),
        ("u_sb", [128, 64], F32), ("u_hi", [128, 64], BF16), ("u_lo", [128, 64], BF16),
        ("u_f", [128, 64], F32), ("rs_fc", [128, 4], F32), ("sfc2", [128, 32], F32),
        ("uT_hi", [16, JB], BF16), ("uT_lo", [16, JB], BF16),
        ("sfc_m", [128, 32], F32), ("sfc_c", [128, 32], F32),
        ("thp_fc", [128, 32], F32), ("thn_fc", [128, 32], F32),
        ("spj_h0", [128, DC], F32), ("spj_m", [128, DC], F32), ("spj_c", [128, DC], F32),
        ("thp_pj", [128, DC], F32), ("thn_pj", [128, DC], F32),
        ("xvb_hi", [16, 512], BF16), ("xvb_lo", [16, 512], BF16),
        ("bv_g", [1, 256], F32),
    ):
        sm[name] = nc.alloc_sbuf_tensor(name, shp, dt).ap()

    st2 = {k: sm[k][:, 0:16] for k in ("m_raw", "mS", "nmS", "s2", "msq", "e1",
                                       "veps", "stdv", "rr", "nmr")}


    PS = [nc.alloc_psum_tensor(f"ps{i}", [128, 512], F32).ap() for i in range(6)]
    psT_banks = [nc.alloc_psum_tensor(f"psT{i}", [128, 1024], BF16).ap()
                 for i in range(2)]

    p = Plan()
    pool = Pool(p, PS)
    psT_guards = [["V", 0], ["V", 0]]
    psT_rr = [0]
    tb_tv = [0]
    idn, idnf = sm["idn"], sm["idnf"]


    def rows3(src2d, r0, ntiles, width):
        return src2d[r0:r0 + ntiles * 128, :].rearrange("(n p) d -> p n d", p=128)

    def dst3(ap2d, width):
        return ap2d.rearrange("p (n d) -> p n d", d=width)

    def dma(out_ap, in_ap, waits=(), slow=False, sem="dL"):
        for (s, v) in waits:
            p.wait("sync", s, v)
        return p.ins("sync", lambda e, o=out_ap, i=in_ap, sl=slow: e.dma_start(
            out=o, in_=i, allow_slow_non_contiguous=sl), sem, 16)

    def tbatch(srcs, dst, idn_ap):
        bi = psT_rr[0]
        psT_rr[0] = 1 - bi
        psT = psT_banks[bi]
        g_ = psT_guards[bi]
        p.wait("tensor", g_[0], g_[1])
        sp, sf = srcs[0].shape[0], srcs[0].shape[1]
        n = len(srcs)
        tv = None
        for k, src in enumerate(srcs):
            fn = lambda e, s=src, o=k * sp, w=sp, pr=sf, ii=idn_ap, ps_=psT: e.transpose(
                ps_[0:pr, o:o + w], s, ii)
            if k == n - 1:
                tv = p.ins("tensor", fn, "T", 1)
            else:
                p.ins("tensor", fn)
        tb_tv[0] = tv
        p.wait("vector", "T", tv)
        vv = p.ins("vector", lambda e, d=dst, pr=sf, w=n * sp, ps_=psT: e.tensor_copy(
            out=d, in_=ps_[0:pr, 0:w]), "V", 1)
        g_[0], g_[1] = "V", vv
        return vv

    def seg_rhs(base, tt0, ntiles, dc):
        seg = HA[:, base + tt0 * 1024: base + (tt0 + ntiles) * 1024]
        return seg.rearrange("p (i x) -> p i x", x=1024)[:, :, dc * 128:(dc + 1) * 128]

    def wvt_rhs(hl, dc):
        seg = HA[:, h_WT(2, hl): h_WT(2, hl) + 2048]
        return seg.rearrange("p (i x) -> p i x", x=1024)[:, :, dc * 128:(dc + 1) * 128]

    dma(FA[0:8, 21632:21760], bq_h)
    dma(FA[0:8, 21760:21888], bk_h)
    dma(sm["gate_s"], gate_h)
    dma(f_vsb(), v_h)
    dma(sm["fb_sb"], fb_h)
    ld0 = dma(dst3(FA[:, 0:8192], 1024), rows3(xo_h, 0, 8, 1024))
    ldA = ld0

    p.ins("vector", lambda e: e.memset(sm["onescol"], 1.0))
    p.ins("vector", lambda e: e.memset(h_ones128(), 1.0))
    vones = p.ins("vector", lambda e: e.memset(f_ones128f(), 1.0), "V", 1)
    p.wait("gpsimd", "V", vones)
    p.ins("gpsimd", lambda e: e.affine_select(sm["idn"], h_ones128(), [[1, 128]],
                                              ALU.is_equal, 0.0, channel_multiplier=-1))
    gA = p.ins("gpsimd", lambda e: e.affine_select(sm["idnf"], f_ones128f(), [[1, 128]],
                                                   ALU.is_equal, 0.0,
                                                   channel_multiplier=-1), "G", 1)
    p.wait("vector", "dL", ldA)
    p.ins("vector", lambda e: e.tensor_copy(out=h_vhi(), in_=f_vsb()))
    p.ins("vector", lambda e: e.tensor_copy(out=sm["fb_hi"], in_=sm["fb_sb"]))
    p.ins("vector", lambda e: e.drain())
    p.ins("vector", lambda e: e.tensor_tensor(h_vlo(), f_vsb(), h_vhi(), ALU.subtract))
    vA = p.ins("vector", lambda e: e.tensor_sub(sm["fb_lo"], sm["fb_sb"], sm["fb_hi"]),
               "V", 1)
    p.wait("tensor", "V", vA)
    p.wait("tensor", "G", gA)
    st = pool.acquire()
    tv = p.ins("tensor", lambda e, b=st["ap"]: e.matmul(
        b[:, 0:1], sm["onescol"], sm["gate_s"], start=True, stop=True), "T", 1)
    p.wait("vector", "T", tv)
    vv = p.ins("vector", lambda e, b=st["ap"]: e.tensor_copy(out=sm["gate_bc"],
                                                             in_=b[:, 0:1]), "V", 1)
    pool.release(st, "V", vv)

    stb = pool.acquire()
    p.ins("tensor", lambda e, b=stb["ap"]: e.transpose(
        b[0:128, 0:8], FA[0:8, 21632:21760], idnf[0:8, 0:8]))
    tvb = p.ins("tensor", lambda e, b=stb["ap"]: e.transpose(
        b[0:128, 8:16], FA[0:8, 21760:21888], idnf[0:8, 0:8]), "T", 1)
    p.wait("vector", "T", tvb)
    p.ins("vector", lambda e, b=stb["ap"]: e.tensor_copy(out=sm["bq_pp"],
                                                         in_=b[:, 0:8]))
    vvb = p.ins("vector", lambda e, b=stb["ap"]: e.tensor_copy(out=sm["bk_pp"],
                                                               in_=b[:, 8:16]), "V", 1)
    pool.release(stb, "V", vvb)
    p.ins("vector", lambda e: e.drain())
    p.ins("vector", lambda e: e.tensor_scalar(sm["bqs"], sm["bq_pp"], ZS, 0.0,
                                              ALU.mult, ALU.add))

    tbatch([h_vhi()[:, dc * 128:(dc + 1) * 128] for dc in range(DC)],
           sm["vT_hi"], idn[0:16, 0:16])
    tbatch([h_vlo()[:, dc * 128:(dc + 1) * 128] for dc in range(DC)],
           sm["vT_lo"], idn[0:16, 0:16])
    tbatch([sm["fb_hi"]], sm["fbT_hi"], idn[0:16, 0:16])
    tbatch([sm["fb_lo"]], sm["fbT_lo"], idn[0:16, 0:16])

    def stats_pair(x0, x1, col0, m_raw, s2):
        p.wait("scalar", "V", p.val("V"))
        sqv = None
        for i, xs in enumerate((x0, x1)):
            sqv = p.ins("scalar", lambda e, x=xs, i_=i: e.activation(
                out=f_xnslab(i_), in_=x, func=AF.Square, scale=1.0), "S", 1)
        p.wait("vector", "S", sqv)
        for i, xs in enumerate((x0, x1)):
            p.ins("vector", lambda e, x=xs, c=col0 + i, m=m_raw: e.tensor_reduce(
                m[:, c:c + 1], x, AX.X, ALU.add))
        vv = None
        for i in range(2):
            vv = p.ins("vector", lambda e, i_=i, c=col0 + i, s=s2: e.tensor_reduce(
                s[:, c:c + 1], f_xnslab(i_), AX.X, ALU.add), "V", 1)
        return vv

    def statmath(t):
        p.ins("vector", lambda e: e.drain())
        p.ins("vector", lambda e: e.tensor_scalar(t["mS"], t["m_raw"], 1.0 / D, 0.0,
                                                  ALU.mult, ALU.add))
        p.ins("vector", lambda e: e.tensor_scalar(t["e1"], t["s2"], 1.0 / D, EPS,
                                                  ALU.mult, ALU.add))
        p.ins("vector", lambda e: e.drain())
        p.ins("vector", lambda e: e.tensor_mul(t["msq"], t["mS"], t["mS"]))
        p.ins("vector", lambda e: e.tensor_scalar(t["nmS"], t["mS"], -1.0, 0.0,
                                                  ALU.mult, ALU.add))
        p.ins("vector", lambda e: e.drain())
        sv = p.ins("vector", lambda e: e.tensor_sub(t["veps"], t["e1"], t["msq"]), "V", 1)
        p.wait("scalar", "V", sv)
        ssv = p.ins("scalar", lambda e: e.activation(out=t["stdv"], in_=t["veps"],
                                                     func=AF.Sqrt, scale=1.0), "S", 1)
        p.wait("vector", "S", ssv)
        p.ins("vector", lambda e: e.reciprocal(t["rr"], t["stdv"]))
        p.ins("vector", lambda e: e.drain())
        return p.ins("vector", lambda e: e.tensor_mul(t["nmr"], t["nmS"], t["rr"]), "V", 1)

    fm_last_spill = [0]

    def emit_slab512(src_h, row0, cols4, r_, nmr_, t_guard, v_guard=None,
                     spill_idx=None):
        vv = None
        first_ev = True
        t_rnd0 = 0
        sp_guard = fm_last_spill[0]
        for rnd in range(2):
            ld = dma(dst3(FA[:, 16384:18432], 1024),
                     rows3(src_h, row0 + rnd * 256, 2, 1024),
                     waits=tuple(w for w in (("S", p.val("S")),
                                             ("V", v_guard if rnd == 0 else None))
                                 if w[1]))
            p.wait("scalar", "dL", ld)
            p.wait("scalar", "G", p.val("G"))
            sv = None
            for i in range(2):
                c = cols4[rnd * 2 + i]
                sv = p.ins("scalar", lambda e, i_=i, c_=c: e.activation(
                    out=f_xnslab(i_), in_=f_xslab(i_), func=AF.Identity,
                    bias=nmr_[:, c_:c_ + 1], scale=r_[:, c_:c_ + 1]), "S", 1)
            p.ins("scalar", lambda e: e.drain())
            p.wait("scalar", "T", t_guard if rnd == 0 else t_rnd0)
            cv = None
            for i in range(2):
                cv = p.ins("scalar", lambda e, i_=i: e.activation(
                    out=h_xnhi(i_), in_=f_xnslab(i_), func=AF.Identity,
                    scale=1.0), "S", 1)
            p.wait("gpsimd", "S", cv)
            p.wait("gpsimd", "T", t_guard if rnd == 0 else t_rnd0)
            gv = None
            for i in range(2):
                gv = p.ins("gpsimd", lambda e, i_=i: e.tensor_tensor(
                    h_xnlo(i_), f_xnslab(i_), h_xnhi(i_), ALU.subtract), "G", 1)
            p.wait("tensor", "G", gv)
            for tm, fmb in ((h_xnhi, SLABFM_HI), (h_xnlo, SLABFM_LO)):
                for i in range(2):
                    tt = rnd * 2 + i
                    srcs = [tm(i)[:, dc * 128:(dc + 1) * 128] for dc in range(DC)]
                    if first_ev:
                        if t_guard:
                            p.wait("vector", "T", t_guard)
                        p.wait("vector", "dS", sp_guard)
                    first_ev = False
                    vv = tbatch(srcs, HA[:, fmb + tt * 1024: fmb + (tt + 1) * 1024], idn)
            t_rnd0 = tb_tv[0]
        if spill_idx is not None:
            fm_last_spill[0] = dma(
                fmsp_h[:, spill_idx * 8192:(spill_idx + 1) * 8192],
                HA[:, SLABFM_HI:SLABFM_LO + 4096], waits=(("V", vv),), sem="dS")
        return vv

    def reload_slab(idx, t_guard, extra=()):
        return dma(HA[:, SLABFM_HI:SLABFM_LO + 4096],
                   fmsp_h[:, idx * 8192:(idx + 1) * 8192],
                   waits=(("T", t_guard), ("dS", fm_last_spill[0])) + tuple(extra))

    def prep_weight(wk, g, wslot, t_guard, v_guard, stage=16384, dma_waits=None,
                    defer_evac=False):
        if wk != "o":
            src = rows3(w_h[wk], 256 * g, 2, 1024)
            wdw = 1024
        else:
            src = w_h["o"][:, 256 * g:256 * (g + 1)].rearrange(
                "(c p) d -> p c d", p=128)
            wdw = 256
        wbuf = FA[:, stage:stage + 2048]
        ld = dma(dst3(wbuf, wdw), src,
                 waits=(dma_waits if dma_waits is not None else
                        tuple(w for w in (("T", t_guard), ("S", p.val("S")),
                                          ("V", p.val("V"))) if w[1])))
        p.wait("vector", "dL", ld)
        if v_guard:
            p.wait("vector", "T", v_guard)
        sc = ZS if wk == "q" else 1.0
        p.ins("vector", lambda e, s=sc, b=wbuf: e.tensor_scalar(
            h2_wsphi(), b, s, 0.0, ALU.mult, ALU.add))
        p.ins("vector", lambda e: e.drain())
        vv = p.ins("vector", lambda e, s=sc, b=wbuf: e.scalar_tensor_tensor(
            h2_wsplo(), b, s, h2_wsphi(), ALU.mult, ALU.subtract),
            "V", 1)
        def _evac():
            p.wait("tensor", "V", vv)
            for hl, wsp in enumerate((h2_wsphi(), h2_wsplo())):
                if wk != "o":
                    for ecl in range(2):
                        srcs = [wsp[:, ecl * 1024 + dc * 128:
                                    ecl * 1024 + (dc + 1) * 128]
                                for dc in range(DC)]
                        tbatch(srcs, HA[:, h_WT(wslot, hl) + ecl * 1024:
                                        h_WT(wslot, hl) + (ecl + 1) * 1024], idn)
                else:
                    for dcl in range(2):
                        srcs = [wsp[:, ec * 256 + dcl * 128:
                                    ec * 256 + (dcl + 1) * 128]
                                for ec in range(DC)]
                        tbatch(srcs, HA[:, WOT_BASE + hl * 2048 + dcl * 1024:
                                        WOT_BASE + hl * 2048 + (dcl + 1) * 1024],
                               idn)
        if defer_evac:
            return _evac
        _evac()

    stv = {k: sm[k][:, 0:16] for k in ("m_raw", "mS", "nmS", "s2", "msq", "e1",
                                       "veps", "stdv", "rr", "nmr")}
    sto = {k: sm[k][:, 16:32] for k in ("m_raw", "mS", "nmS", "s2", "msq", "e1",
                                        "veps", "stdv", "rr", "nmr")}
    sq_own = []
    ld1 = None
    p.wait("scalar", "dL", ld0)
    for j in range(16):
        if j == 8:
            p.wait("scalar", "dL", ld1)
            p.wait("vector", "dL", ld1)
        if j >= 2:
            p.ins("scalar", lambda e: e.drain())
        sq = p.ins("scalar", lambda e, j_=j: e.activation(
            out=f_xnslab(j_ % 2), in_=f_xres_t(j_), func=AF.Square, scale=1.0,
            accum_out=sm["s2"][:, j_:j_ + 1]), "S", 1)
        sq_own.append(sq)
        p.ins("vector", lambda e, j_=j: e.tensor_reduce(
            sm["m_raw"][:, j_:j_ + 1], f_xres_t(j_), AX.X, ALU.add), "V", 1)
        if j == 0:
            ld1 = dma(dst3(FA[:, 8192:16384], 1024), rows3(xo_h, 1024, 8, 1024),
                      waits=(("S", sq),))
        if j == 8:
            prep_weight("q", 0, 0, None, None, stage=23552,
                        dma_waits=(("S", sq),))
    p.wait("vector", "S", sq_own[-1])
    ln1a = statmath(stv)
    p.wait("scalar", "V", ln1a)

    slab_tg = p.val("T")
    kt_tg = None
    v_tg = None
    wt_tg = None
    attn_tg = [None] * 4

    for g in range(G):
        wprep_vg = p.val("V")

        sqx = None
        for sl in range(4):
            if g == 0:
                vv = emit_slab512(xo_h, sl * 512,
                                  tuple(sl * 4 + i for i in range(4)),
                                  sm["rr"], sm["nmr"], slab_tg,
                                  wprep_vg if sl == 0 else p.val("V"),
                                  spill_idx=sl)
                p.wait("tensor", "V", vv)
            else:
                ld = reload_slab(sl, slab_tg,
                                 ((("V", wprep_vg),) if sl == 0 else ()))
                if sl == 0:
                    p.wait("tensor", "V", wprep_vg)
                p.wait("tensor", "dL", ld)
            banks = []
            tqv = None
            for pc in range(2):
                stq = pool.acquire()
                n_mm = [0]
                for (whl, xhl) in ((0, 0), (1, 0), (0, 1)):
                    for dc in range(DC):
                        n_mm[0] += 1
                        fn = lambda e, b=stq["ap"], w_=whl, x_=xhl, d=dc, pc_=pc, \
                            first=(n_mm[0] == 1), last=(n_mm[0] == 24): e.matmul(
                            b,
                            HA[:, h_WT(0, w_) + pc_ * 1024 + d * 128:
                               h_WT(0, w_) + pc_ * 1024 + (d + 1) * 128],
                            seg_rhs(SLABFM_HI if x_ == 0 else SLABFM_LO, 0, 4, d),
                            start=first, stop=last)
                        if n_mm[0] == 24:
                            tqv = p.ins("tensor", fn, "T", 1)
                        else:
                            p.ins("tensor", fn)
                banks.append(stq)
            p.wait("vector", "T", tqv)
            qv = None
            for pc, stq in enumerate(banks):
                c = 2 * g + pc
                qv = p.ins("vector", lambda e, b=stq["ap"], pc_=pc, c_=c, s=sl:
                           e.tensor_scalar(
                    h_QThi(pc_)[:, s * 512:(s + 1) * 512], b,
                    sm["bqs"][:, c_:c_ + 1], None, ALU.add), "V", 1)
            p.wait("vector", "V", qv)
            vv2 = None
            for pc, stq in enumerate(banks):
                c = 2 * g + pc
                vv2 = p.ins("vector", lambda e, b=stq["ap"], pc_=pc, c_=c, s=sl:
                            e.scalar_tensor_tensor(
                    h_QTlo(pc_)[:, s * 512:(s + 1) * 512], b,
                    sm["bqs"][:, c_:c_ + 1], h_QThi(pc_)[:, s * 512:(s + 1) * 512],
                    ALU.add, ALU.subtract), "V", 1)
            for stq in banks:
                pool.release(stq, "V", vv2)
            slab_tg = p.val("T")

        dma(sm["bv_g"], bv_h[:, 256 * g:256 * (g + 1)],
            waits=(("T", slab_tg), ("S", p.val("S"))))
        prep_weight("k", g, 0, slab_tg, slab_tg)
        prep_weight("v", g, 2, None, None)
        prep_weight("o", g, 3, None, None)
        slab_tg = p.val("T")
        wprep_vg = p.val("V")

        for kb in range(NKB):
            own = kb < 4
            src = xo_h if own else xt_h
            row0 = (kb if own else kb - 4) * 512
            ccol0 = (kb * 4) if own else (16 + (kb - 4) * 4)
            if g == 0 and kb == 4:
                p.wait("vector", "S", sqx)
                ln1b = statmath(sto)
                p.wait("scalar", "V", ln1b)
            if g == 0 and kb >= 4:
                vv = emit_slab512(src, row0, tuple(ccol0 + i for i in range(4)),
                                  sm["rr"], sm["nmr"], slab_tg,
                                  p.val("V") if kb == 4 else None,
                                  spill_idx=kb)
                p.wait("tensor", "V", vv)
            else:
                ld = reload_slab(kb, slab_tg,
                                 ((("V", wprep_vg),) if kb == 0 else
                                  ((("S", p.val("S")),) if g == 0 else ())))
                if kb == 0:
                    p.wait("tensor", "V", wprep_vg)
                p.wait("tensor", "dL", ld)
            kbanks = []
            tkv = None
            for pc in range(2):
                stk = pool.acquire()
                n_mm = [0]
                for (whl, xhl) in ((0, 0), (1, 0), (0, 1)):
                    for dc in range(DC):
                        n_mm[0] += 1
                        fn = lambda e, b=stk["ap"], w_=whl, x_=xhl, d=dc, pc_=pc, \
                            first=(n_mm[0] == 1), last=(n_mm[0] == 24): e.matmul(
                            b,
                            HA[:, h_WT(0, w_) + pc_ * 1024 + d * 128:
                               h_WT(0, w_) + pc_ * 1024 + (d + 1) * 128],
                            seg_rhs(SLABFM_HI if x_ == 0 else SLABFM_LO, 0, 4, d),
                            start=first, stop=last)
                        if n_mm[0] == 24:
                            tkv = p.ins("tensor", fn, "T", 1)
                        else:
                            p.ins("tensor", fn)
                kbanks.append(stk)
            p.wait("vector", "T", tkv)
            if kt_tg is not None:
                p.wait("vector", "T", kt_tg)
                kt_tg = None
            kv = None
            for pc, stk in enumerate(kbanks):
                c = 2 * g + pc
                kv = p.ins("vector", lambda e, b=stk["ap"], pc_=pc, c_=c:
                           e.tensor_scalar(
                    h2_KThi(pc_), b, sm["bk_pp"][:, c_:c_ + 1], None, ALU.add),
                    "V", 1)
            p.wait("vector", "V", kv)
            vv2 = None
            for pc, stk in enumerate(kbanks):
                c = 2 * g + pc
                vv2 = p.ins("vector", lambda e, b=stk["ap"], pc_=pc, c_=c:
                            e.scalar_tensor_tensor(
                    h2_KTlo(pc_), b, sm["bk_pp"][:, c_:c_ + 1], h2_KThi(pc_),
                    ALU.add, ALU.subtract), "V", 1)
            for stk in kbanks:
                pool.release(stk, "V", vv2)
            for kt in range(4):
                stv = pool.acquire()
                tvv = None
                n_mm = [0]
                p.ins("tensor", lambda e, b=stv["ap"]: e.matmul(
                    b[:, 0:256], sm["onescol"], sm["bv_g"], start=True, stop=False))
                for (whl, xhl) in ((0, 0), (0, 1), (1, 0)):
                    for dc in range(DC):
                        n_mm[0] += 1
                        base = SLABFM_HI if xhl == 0 else SLABFM_LO
                        fn = lambda e, b=stv["ap"], w_=whl, d=dc, i_=kt, bb=base, \
                            last=(n_mm[0] == 24): e.matmul(
                            b[:, 0:256],
                            HA[:, bb + i_ * 1024 + d * 128:
                               bb + i_ * 1024 + (d + 1) * 128],
                            wvt_rhs(w_, d),
                            start=False, stop=last)
                        if n_mm[0] == 24:
                            tvv = p.ins("tensor", fn, "T", 1)
                        else:
                            p.ins("tensor", fn)
                p.wait("vector", "T", tvv)
                if v_tg is not None:
                    p.wait("vector", "T", v_tg)
                    v_tg = None
                vev = p.ins("vector", lambda e, b=stv["ap"], k=kt: e.tensor_copy(
                    out=f_V(k), in_=b[:, 0:256]), "V", 1)
                pool.release(stv, "V", vev)
            slab_tg = p.val("T")

            xt_jobs = list(range(kb * 4, kb * 4 + 4)) if (g == 0 and kb < 4) else []
            if xt_jobs:
                p.wait("scalar", "G", p.val("G"))

            def emit_xt_tile(j):
                nonlocal sqx
                ldx = dma(f_xslab(j % 2), xt_h[j * 128:(j + 1) * 128, :],
                          waits=(("S", p.val("S")), ("V", p.val("V")),
                                 ("T", slab_tg)))
                p.wait("scalar", "dL", ldx)
                p.ins("scalar", lambda e: e.drain())
                sqx = p.ins("scalar", lambda e, j_=j: e.activation(
                    out=f_xnslab(j_ % 2), in_=f_xslab(j_ % 2),
                    func=AF.Square, scale=1.0,
                    accum_out=sm["s2"][:, 16 + j_:17 + j_]), "S", 1)
                p.wait("vector", "dL", ldx)
                p.ins("vector", lambda e, j_=j: e.tensor_reduce(
                    sm["m_raw"][:, 16 + j_:17 + j_], f_xslab(j_ % 2),
                    AX.X, ALU.add), "V", 1)

            abuf = [0]
            for qt in range(NQT):
                for pc in range(2):
                    if xt_jobs and (qt * 2 + pc) % 2 == 1:
                        emit_xt_tile(xt_jobs.pop(0))
                    pvA = pool.acquire()
                    pvB = pool.acquire()
                    tpv = None
                    pend = []
                    def emit_pv(ent, first, last):
                        nonlocal tpv
                        bufA_, bufB_, sv_, kt_ = ent
                        p.wait("tensor", "S", sv_)
                        p.ins("tensor", lambda e, b=pvA["ap"], k=kt_, pc_=pc, o=bufA_,
                              s_=first, t_=last: e.matmul(
                            b[0:64, :], f_V(k)[:, pc_ * 128: pc_ * 128 + 64], f_attn(o),
                            start=s_, stop=t_, tile_position=(0, 0)))
                        tpv = p.ins("tensor", lambda e, b=pvB["ap"], k=kt_, pc_=pc,
                                    o=bufB_, s_=first, t_=last: e.matmul(
                            b[64:128, :], f_V(k)[:, pc_ * 128 + 64: pc_ * 128 + 128],
                            f_attn(o),
                            start=s_, stop=t_, tile_position=(0, 64)), "T", 1)
                        attn_tg[bufA_] = tpv
                        attn_tg[bufB_] = tpv
                    for kt in range(4):
                        zA = pool.acquire()
                        zB = pool.acquire()
                        tz = None
                        for pi, (khl, qhl) in enumerate(((0, 0), (1, 0), (0, 1))):
                            KT_ = h2_KThi if khl == 0 else h2_KTlo
                            QT_ = h_QThi if qhl == 0 else h_QTlo
                            p.ins("tensor", lambda e, b=zA["ap"], K_=KT_, Q_=QT_, pc_=pc,
                                  k=kt, q=qt, s_=(pi == 0), t_=(pi == 2): e.matmul(
                                b, K_(pc_)[0:64, k * 128:(k + 1) * 128],
                                Q_(pc_)[0:64, q * 512:(q + 1) * 512],
                                start=s_, stop=t_, tile_position=(0, 0)))
                            fnB = lambda e, b=zB["ap"], K_=KT_, Q_=QT_, pc_=pc, \
                                k=kt, q=qt, s_=(pi == 0), t_=(pi == 2): e.matmul(
                                b, K_(pc_)[64:128, k * 128:(k + 1) * 128],
                                Q_(pc_)[64:128, q * 512:(q + 1) * 512],
                                start=s_, stop=t_, tile_position=(64, 0))
                            if pi == 2:
                                tz = p.ins("tensor", fnB, "T", 1)
                            else:
                                p.ins("tensor", fnB)
                        p.wait("scalar", "T", tz)
                        bufA, bufB = abuf[0] % 4, (abuf[0] + 1) % 4
                        abuf[0] += 2
                        for bf in (bufA, bufB):
                            if attn_tg[bf] is not None:
                                p.wait("scalar", "T", attn_tg[bf])
                                attn_tg[bf] = None
                        p.ins("scalar", lambda e, b=zA["ap"], o=bufA: e.activation(
                            out=f_attn(o), in_=b, func=AF.Sin, scale=1.0))
                        sv = p.ins("scalar", lambda e, b=zB["ap"], o=bufB: e.activation(
                            out=f_attn(o), in_=b, func=AF.Sin, scale=1.0), "S", 1)
                        pool.release(zA, "S", sv)
                        pool.release(zB, "S", sv)
                        pend.append((bufA, bufB, sv, kt))
                        if kt >= 1:
                            ent = pend.pop(0)
                            emit_pv(ent, ent[3] == 0, ent[3] == 3)
                    while pend:
                        ent = pend.pop(0)
                        emit_pv(ent, ent[3] == 0, ent[3] == 3)
                    p.wait("vector", "T", tpv)
                    if kb == 0:
                        p.ins("vector", lambda e, b=pvA["ap"], pc_=pc, q=qt:
                              e.tensor_copy(
                            out=f_accatt(pc_)[0:64, q * 512:(q + 1) * 512],
                            in_=b[0:64, :]))
                        vacc = p.ins("vector", lambda e, b=pvB["ap"], pc_=pc, q=qt:
                                     e.tensor_copy(
                            out=f_accatt(pc_)[64:128, q * 512:(q + 1) * 512],
                            in_=b[64:128, :]), "V", 1)
                    else:
                        p.ins("vector", lambda e, b=pvA["ap"], pc_=pc, q=qt:
                              e.tensor_add(
                            f_accatt(pc_)[0:64, q * 512:(q + 1) * 512],
                            f_accatt(pc_)[0:64, q * 512:(q + 1) * 512], b[0:64, :]))
                        vacc = p.ins("vector", lambda e, b=pvB["ap"], pc_=pc, q=qt:
                                     e.tensor_add(
                            f_accatt(pc_)[64:128, q * 512:(q + 1) * 512],
                            f_accatt(pc_)[64:128, q * 512:(q + 1) * 512],
                            b[64:128, :]), "V", 1)
                    pool.release(pvA, "V", vacc)
                    pool.release(pvB, "V", vacc)
            v_tg = p.val("T")
            kt_tg = p.val("T")

        wq_evac = None
        if g < G - 1:
            wq_evac = prep_weight("q", g + 1, 0, kt_tg, kt_tg, defer_evac=True)

        if g == 0:
            bo_ld = dma(f_bo(), bo_h, waits=(("T", slab_tg), ("S", p.val("S")),
                                             ("V", p.val("V"))))
            p.wait("tensor", "dL", bo_ld)
        p.wait("vector", "T", slab_tg)
        p.wait("vector", "dS", fm_last_spill[0])
        p.ins("vector", lambda e: e.drain())
        for pc in range(2):
            p.ins("vector", lambda e, pc_=pc: e.tensor_copy(
                out=h_accThi(pc_), in_=f_accatt(pc_)))
        p.ins("vector", lambda e: e.drain())
        spl = None
        for pc in range(2):
            spl = p.ins("vector", lambda e, pc_=pc: e.tensor_sub(
                h_accTlo(pc_), f_accatt(pc_), h_accThi(pc_)), "V", 1)
        p.wait("tensor", "V", spl)
        if g == G - 1:
            spl3 = spl
            spill_tiles = []
        for tt in range(16):
            for eh in range(2):
                sto = pool.acquire()
                ntot = 6 + (1 if g == 0 else 0)
                n_mm = [0]
                tov = None
                if g == 0:
                    n_mm[0] += 1
                    p.ins("tensor", lambda e, b=sto["ap"], e_=eh: e.matmul(
                        b, sm["onescol"], f_bo()[:, e_ * 512:(e_ + 1) * 512],
                        start=True, stop=False))
                for (ahl, whl) in ((0, 0), (1, 0), (0, 1)):
                    for pc in range(2):
                        n_mm[0] += 1
                        aT = h_accThi if ahl == 0 else h_accTlo
                        fn = lambda e, b=sto["ap"], a_=aT, pc_=pc, w_=whl, t=tt, e_=eh, \
                            first=(n_mm[0] == 1 and g != 0), last=(n_mm[0] == ntot): \
                            e.matmul(
                            b, a_(pc_)[:, t * 128:(t + 1) * 128],
                            HA[:, WOT_BASE + w_ * 2048 + pc_ * 1024 + e_ * 512:
                               WOT_BASE + w_ * 2048 + pc_ * 1024 + (e_ + 1) * 512],
                            start=first, stop=last)
                        if n_mm[0] == ntot:
                            tov = p.ins("tensor", fn, "T", 1)
                        else:
                            p.ins("tensor", fn)
                p.wait("vector", "T", tov)
                vo = p.ins("vector", lambda e, b=sto["ap"], t=tt, e_=eh: e.tensor_add(
                    f_xres_t(t)[:, e_ * 512:(e_ + 1) * 512],
                    f_xres_t(t)[:, e_ * 512:(e_ + 1) * 512], b), "V", 1)
                pool.release(sto, "V", vo)
            if g == G - 1:
                p.wait("sync", "V", vo)
                spill_tiles.append(p.ins("sync", lambda e, t=tt: e.dma_start(
                    out=xspill_h[:, t * 1024:(t + 1) * 1024], in_=f_xres_t(t)),
                    "dS", 16))
            if g == G - 1 and tt % 2 == 1:
                stats_pair(f_xres_t(tt - 1), f_xres_t(tt), tt - 1,
                           st2["m_raw"], st2["s2"])
                if tt == 15:
                    ln2_v = statmath(st2)
        if wq_evac is not None:
            wq_evac()
        wt_tg = p.val("T")
        slab_tg = p.val("T")

    p.wait("scalar", "V", ln2_v)

    ha_free_t = p.val("T")
    p.wait("gpsimd", "V", spl3)
    ldg0 = p.ins("gpsimd", lambda e: e.dma_start(
        out=dst3(f_w0(), 1024), in_=rows3(wfc_h, 0, 2, 1024)), "dG", 16)
    p.wait("scalar", "dG", ldg0)
    sqf0 = None
    for j in range(2):
        p.ins("scalar", lambda e: e.drain())
        sqf0 = p.ins("scalar", lambda e, j_=j: e.activation(
            out=A(26624, 1024), in_=f_w0()[:, j_ * 1024:(j_ + 1) * 1024],
            func=AF.Abs, scale=1.0 / D,
            accum_out=sm["sfc_m"][:, j_:j_ + 1]), "S", 1)

    spj_O = A(20480, 2048)
    spj_L = (A(23552, 2048), A(25600, 2048))
    sqsc = [None]
    redsc = [None]

    def scale_iter(i):
        ec, hf = i // 2, i % 2
        Lb = spj_L[i % 2]
        w = (("V", spl3),) if i == 0 else (("S", sqsc[0]),)
        ld = dma(Lb, wpj_h[ec * 128:(ec + 1) * 128, hf * 2048:(hf + 1) * 2048],
                 waits=w)
        p.wait("scalar", "dL", ld)
        if i == 0:
            p.wait("scalar", "T", ha_free_t)
        else:
            p.wait("scalar", "V", redsc[0])
        sqsc[0] = p.ins("scalar", lambda e, b=Lb: e.activation(
            out=spj_O, in_=b, func=AF.Abs, scale=1.0 / FF), "S", 1)
        p.wait("vector", "S", sqsc[0])
        dstc = sm["spj_h0"] if hf == 0 else sm["spj_m"]
        redsc[0] = p.ins("vector", lambda e, c=ec, d=dstc: e.tensor_reduce(
            d[:, c:c + 1], spj_O, AX.X, ALU.add), "V", 1)

    def emit_lr(tq, x2l):
        stx = pool.acquire()
        p.wait("tensor", "V", x2l)
        n_mm = [0]
        txv = None
        for (vhl, xhl) in ((0, 0), (1, 0), (0, 1)):
            vT = sm["vT_hi"] if vhl == 0 else sm["vT_lo"]
            for dc in range(DC):
                n_mm[0] += 1
                fn = lambda e, b=stx["ap"], v_=vT, d=dc, x_=xhl, q=tq, \
                    first=(n_mm[0] == 1), last=(n_mm[0] == 24): e.matmul(
                    b[0:16, :], v_[:, d * 16:(d + 1) * 16],
                    seg_rhs(X2T_HI if x_ == 0 else X2T_LO, q * 4, 4, d),
                    start=first, stop=last)
                if n_mm[0] == 24:
                    txv = p.ins("tensor", fn, "T", 1)
                else:
                    p.ins("tensor", fn)
        return (stx, txv)

    c0b_tg = ha_free_t
    vv = None
    lr_banks = []
    for sl in range(8):
        p.wait("scalar", "G", p.val("G"))
        sv = None
        for i in range(2):
            c = sl * 2 + i
            sv = p.ins("scalar", lambda e, i_=i, c_=c: e.activation(
                out=f_xnslab(i_), in_=f_xres_t(c_), func=AF.Identity,
                bias=st2["nmr"][:, c_:c_ + 1], scale=st2["rr"][:, c_:c_ + 1]), "S", 1)
        p.ins("scalar", lambda e: e.drain())
        p.wait("scalar", "T", c0b_tg)
        cv = None
        for i in range(2):
            cv = p.ins("scalar", lambda e, i_=i: e.activation(
                out=h2_slabhi(i_), in_=f_xnslab(i_), func=AF.Identity,
                scale=1.0), "S", 1)
        p.wait("gpsimd", "S", cv)
        p.wait("gpsimd", "T", c0b_tg)
        gv = None
        for i in range(2):
            gv = p.ins("gpsimd", lambda e, i_=i: e.tensor_tensor(
                h2_slablo(i_), f_xnslab(i_), h2_slabhi(i_), ALU.subtract), "G", 1)
        p.wait("tensor", "G", gv)
        first = (sl == 0)
        x2l = None
        for tm, base in ((h2_slabhi, X2T_HI), (h2_slablo, X2T_LO)):
            for i in range(2):
                tt = sl * 2 + i
                srcs = [tm(i)[:, dc * 128:(dc + 1) * 128] for dc in range(DC)]
                if first:
                    p.wait("vector", "T", ha_free_t)
                    first = False
                x2l = tbatch(srcs, HA[:, base + tt * 1024: base + (tt + 1) * 1024],
                             idn)
        c0b_tg = p.val("T")
        if sl < 6:
            scale_iter(2 * sl)
            scale_iter(2 * sl + 1)
        if sl % 2 == 1:
            lr_banks.append(emit_lr(sl // 2, x2l))
    x2t_v = p.val("V")

    p.wait("vector", "T", p.val("T"))
    for q_, (stx_, txv_) in enumerate(lr_banks):
        p.ins("vector", lambda e, b=stx_["ap"], q=q_: e.tensor_copy(
            out=h2_xvhi()[:, q * 512:(q + 1) * 512], in_=b[0:16, :]))
    p.ins("vector", lambda e: e.drain())
    gv = None
    for q_, (stx_, txv_) in enumerate(lr_banks):
        gv = p.ins("vector", lambda e, b=stx_["ap"], q=q_: e.tensor_tensor(
            h2_xvlo()[:, q * 512:(q + 1) * 512], b[0:16, :],
            h2_xvhi()[:, q * 512:(q + 1) * 512], ALU.subtract), "V", 1)
    for q_, (stx_, txv_) in enumerate(lr_banks):
        pool.release(stx_, "V", gv)
    vv = gv

    p.wait("gpsimd", "G", p.val("G"))
    ldg1 = p.ins("gpsimd", lambda e: e.dma_start(
        out=dst3(f_w1(), 1024), in_=rows3(wfc_h, 256, 2, 1024)), "dG", 16)
    p.wait("scalar", "dG", ldg1)
    p.wait("scalar", "V", redsc[0])
    sqf1 = None
    for j in range(2):
        p.ins("scalar", lambda e: e.drain())
        sqf1 = p.ins("scalar", lambda e, j_=j: e.activation(
            out=spj_O[:, j_ * 1024:(j_ + 1) * 1024],
            in_=f_w1()[:, j_ * 1024:(j_ + 1) * 1024],
            func=AF.Abs, scale=1.0 / D,
            accum_out=sm["sfc_m"][:, 2 + j_:3 + j_]), "S", 1)
    p.wait("vector", "S", sqf1)
    p.ins("vector", lambda e: e.drain())
    p.ins("vector", lambda e: e.tensor_scalar(
        sm["sfc_c"][:, 0:4], sm["sfc_m"][:, 0:4], 1e-5, 0.0, ALU.max, ALU.add))
    p.ins("scalar", lambda e: e.drain())
    for i_sc in range(12, 16):
        scale_iter(i_sc)
    p.ins("vector", lambda e: e.drain())
    p.ins("vector", lambda e: e.tensor_scalar(
        sm["thp_fc"][:, 0:4], sm["sfc_c"][:, 0:4], 0.5, 0.0, ALU.mult, ALU.add))
    p.ins("vector", lambda e: e.tensor_scalar(
        sm["thn_fc"][:, 0:4], sm["sfc_c"][:, 0:4], -0.5, 0.0, ALU.mult, ALU.add))
    p.ins("vector", lambda e: e.drain())

    spill_v = spill_tiles[-1]
    p.wait("vector", "dS", spill_v)
    p.wait("vector", "G", p.val("G"))
    for ec in range(DC):
        p.ins("vector", lambda e, c=ec: e.memset(f_accffn(c), 0.0),
              "V" if ec == DC - 1 else None, 1)
    accinit_v = p.val("V")

    p.ins("vector", lambda e: e.drain())
    p.ins("vector", lambda e: e.tensor_add(sm["spj_m"], sm["spj_m"], sm["spj_h0"]))
    p.ins("vector", lambda e: e.drain())
    p.ins("vector", lambda e: e.tensor_scalar(sm["spj_c"], sm["spj_m"], 1e-5, 0.0,
                                              ALU.max, ALU.add))
    p.ins("vector", lambda e: e.drain())
    p.ins("vector", lambda e: e.tensor_scalar(sm["thp_pj"], sm["spj_c"], 0.5, 0.0,
                                              ALU.mult, ALU.add))
    spj_v = p.ins("vector", lambda e: e.tensor_scalar(sm["thn_pj"], sm["spj_c"], -0.5,
                                                      0.0, ALU.mult, ALU.add), "V", 1)

    p.wait("tensor", "V", vv)
    xvb_w = []
    for tq in range(NQT):
        stx = pool.acquire()
        txb = None
        for pi, (fhl, xhl) in enumerate(((0, 0), (1, 0), (0, 1))):
            fbT = sm["fbT_hi"] if fhl == 0 else sm["fbT_lo"]
            xv = h2_xvhi() if xhl == 0 else h2_xvlo()
            fn = lambda e, b=stx["ap"], f_=fbT, x_=xv, q=tq, first=(pi == 0), \
                last=(pi == 2): e.matmul(
                b[0:16, :], f_, x_[:, q * 512:(q + 1) * 512], start=first, stop=last)
            if pi == 2:
                txb = p.ins("tensor", fn, "T", 1)
            else:
                p.ins("tensor", fn)
        p.wait("vector", "T", txb)
        if xvb_w:
            p.wait("vector", "dS", xvb_w[-1])
        ev = p.ins("vector", lambda e, b=stx["ap"]: e.tensor_scalar(
            xvb_f, b[0:16, :],
            sm["gate_bc"][0:16, :], None, ALU.mult), "V", 1)
        pool.release(stx, "V", ev)
        xvb_w.append(dma(xvb_h[:, tq * 512:(tq + 1) * 512], xvb_f,
                         waits=(("V", ev),), sem="dS"))

    def prefetch_fc(jbn, rbv_guard):
        ld0 = dma(dst3(f_w0(), 1024), rows3(wfc_h, jbn * 512, 2, 1024),
                  waits=(("S", p.val("S")), ("V", p.val("V"))))
        p.wait("scalar", "dL", ld0)
        if rbv_guard:
            p.wait("scalar", "V", rbv_guard)
        sqf = None
        for j in range(2):
            p.ins("scalar", lambda e: e.drain())
            c = jbn * 4 + j
            sqf = p.ins("scalar", lambda e, j_=j, c_=c: e.activation(
                out=A(26624, 1024), in_=f_w0()[:, j_ * 1024:(j_ + 1) * 1024],
                func=AF.Abs, scale=1.0 / D,
                accum_out=sm["sfc_m"][:, c_:c_ + 1]), "S", 1)
        ld1 = dma(dst3(f_w1(), 1024), rows3(wfc_h, jbn * 512 + 256, 2, 1024),
                  waits=(("S", sqf),))
        p.wait("scalar", "dL", ld1)
        for j in range(2):
            p.ins("scalar", lambda e: e.drain())
            c = jbn * 4 + 2 + j
            sqf = p.ins("scalar", lambda e, j_=j, c_=c: e.activation(
                out=A(26624, 1024), in_=f_w1()[:, j_ * 1024:(j_ + 1) * 1024],
                func=AF.Abs, scale=1.0 / D,
                accum_out=sm["sfc_m"][:, c_:c_ + 1]), "S", 1)
        p.wait("vector", "S", sqf)
        c0 = jbn * 4
        p.ins("vector", lambda e: e.drain())
        p.ins("vector", lambda e, c_=c0: e.tensor_scalar(
            sm["sfc_c"][:, c_:c_ + 4], sm["sfc_m"][:, c_:c_ + 4], 1e-5, 0.0,
            ALU.max, ALU.add))
        p.ins("vector", lambda e: e.drain())
        p.ins("vector", lambda e, c_=c0: e.tensor_scalar(
            sm["thp_fc"][:, c_:c_ + 4], sm["sfc_c"][:, c_:c_ + 4], 0.5, 0.0,
            ALU.mult, ALU.add))
        p.ins("vector", lambda e, c_=c0: e.tensor_scalar(
            sm["thn_fc"][:, c_:c_ + 4], sm["sfc_c"][:, c_:c_ + 4], -0.5, 0.0,
            ALU.mult, ALU.add))
        p.ins("vector", lambda e: e.drain())

    xvb_g = [0]
    r2_tg = p.val("T")
    for jb in range(NJB):
        if r2_tg is not None:
            p.wait("vector", "T", r2_tg)
            r2_tg = None
        for pr in (0, 1):
            if pr == 1:
                p.wait("vector", "T", tb_tv[0])
            for j in range(2):
                jt = pr * 2 + j
                c = jb * 4 + jt
                p.ins("vector", lambda e, j_=j, c_=c, pr_=pr: e.tensor_scalar(
                    h2_a(j_), (f_w0() if pr_ == 0 else f_w1())[:, j_ * 1024:
                                                               (j_ + 1) * 1024],
                    sm["thp_fc"][:, c_:c_ + 1], None, ALU.is_gt))
                p.ins("vector", lambda e, j_=j, c_=c, pr_=pr: e.tensor_scalar(
                    h2_b(j_), (f_w0() if pr_ == 0 else f_w1())[:, j_ * 1024:
                                                               (j_ + 1) * 1024],
                    sm["thn_fc"][:, c_:c_ + 1], None, ALU.is_lt))
            p.ins("vector", lambda e: e.drain())
            tern_v = None
            for j in range(2):
                tern_v = p.ins("vector", lambda e, j_=j: e.tensor_tensor(
                    h2_a(j_), h2_a(j_), h2_b(j_), ALU.subtract), "V", 1)
            p.wait("tensor", "V", tern_v)
            for j in range(2):
                jt = pr * 2 + j
                srcs = [h2_a(j)[:, dc * 128:(dc + 1) * 128] for dc in range(DC)]
                tbatch(srcs, H2[:, TFC_BASE + jt * 1024: TFC_BASE + (jt + 1) * 1024], idn)
        tfc_t = p.val("T")

        for hf in range(2):
            ld = dma(dst3(f_w0(), 512),
                     wpj_h[hf * 512:(hf + 1) * 512, jb * 512:(jb + 1) * 512]
                     .rearrange("(c p) d -> p c d", p=128),
                     waits=(("V", p.val("V")), ("T", tfc_t if hf == 0 else 0)))
            p.wait("vector", "dL", ld)
            for el in range(4):
                ec = hf * 4 + el
                p.ins("vector", lambda e, l=el, c=ec: e.tensor_scalar(
                    h2_a5(l), f_w0()[:, l * 512:(l + 1) * 512],
                    sm["thp_pj"][:, c:c + 1], None, ALU.is_gt))
                p.ins("vector", lambda e, l=el, c=ec: e.tensor_scalar(
                    h2_b5(l), f_w0()[:, l * 512:(l + 1) * 512],
                    sm["thn_pj"][:, c:c + 1], None, ALU.is_lt))
            p.ins("vector", lambda e: e.drain())
            ej_v = None
            for el in range(4):
                ej_v = p.ins("vector", lambda e, l=el: e.tensor_tensor(
                    h2_a5(l), h2_a5(l), h2_b5(l), ALU.subtract), "V", 1)
            p.wait("tensor", "V", ej_v)
            for jc in range(4):
                srcs = [h2_a5(el)[:, jc * 128:(jc + 1) * 128] for el in range(4)]
                tbatch(srcs, H3[:, jc * 1024 + hf * 512: jc * 1024 + (hf + 1) * 512], idn)
        tpj_t = p.val("T")
        r2_tg = tpj_t

        ld = dma(dst3(sm["u_sb"], 16),
                 u_h[jb * 512:(jb + 1) * 512, :].rearrange("(c p) d -> p c d", p=128),
                 waits=(("T", p.val("T")),))
        c0 = jb * 4
        p.ins("vector", lambda e: e.drain())
        p.ins("vector", lambda e, c_=c0: e.reciprocal(
            sm["rs_fc"], sm["sfc_c"][:, c_:c_ + 4]))
        p.ins("vector", lambda e, c_=c0: e.tensor_mul(
            sm["sfc2"][:, c_:c_ + 4], sm["sfc_c"][:, c_:c_ + 4],
            sm["sfc_c"][:, c_:c_ + 4]))
        p.wait("vector", "dL", ld)
        p.ins("vector", lambda e: e.drain())
        for jt in range(4):
            p.ins("vector", lambda e, j_=jt: e.tensor_scalar(
                sm["u_f"][:, j_ * 16:(j_ + 1) * 16],
                sm["u_sb"][:, j_ * 16:(j_ + 1) * 16],
                sm["rs_fc"][:, j_:j_ + 1], None, ALU.mult))
        p.ins("vector", lambda e: e.drain())
        p.ins("vector", lambda e: e.tensor_copy(out=sm["u_hi"], in_=sm["u_f"]))
        p.ins("vector", lambda e: e.drain())
        uv = p.ins("vector", lambda e: e.tensor_sub(sm["u_lo"], sm["u_f"], sm["u_hi"]),
                   "V", 1)
        p.wait("tensor", "V", uv)
        tbatch([sm["u_hi"][:, jt * 16:(jt + 1) * 16] for jt in range(4)], sm["uT_hi"], idn)
        tbatch([sm["u_lo"][:, jt * 16:(jt + 1) * 16] for jt in range(4)], sm["uT_lo"], idn)
        p.wait("tensor", "V", p.val("V"))

        def emit_proj(tq_, hi_v, lo_vv):
            p.wait("tensor", "S", hi_v)
            p.wait("tensor", "V", lo_vv)
            for ec in range(DC):
                stp = pool.acquire()
                n_mm = [0]
                tp = None
                for rhl in (0, 1):
                    r2 = h2_r2hi if rhl == 0 else h2_r2lo
                    for jc in range(4):
                        n_mm[0] += 1
                        fn = lambda e, b=stp["ap"], c=ec, j=jc, r_=r2, \
                            first=(n_mm[0] == 1), last=(n_mm[0] == 8): e.matmul(
                            b, H3[:, j * 1024 + c * 128: j * 1024 + (c + 1) * 128],
                            r_(j), start=first, stop=last)
                        if n_mm[0] == 8:
                            tp = p.ins("tensor", fn, "T", 1)
                        else:
                            p.ins("tensor", fn)
                p.wait("vector", "T", tp)
                va = p.ins("vector", lambda e, b=stp["ap"], c=ec, q=tq_:
                           e.scalar_tensor_tensor(
                    f_accffn(c)[:, q * 512:(q + 1) * 512], b,
                    sm["spj_c"][:, c:c + 1],
                    f_accffn(c)[:, q * 512:(q + 1) * 512], ALU.mult, ALU.add), "V", 1)
                pool.release(stp, "V", va)

        hi_s = lo_v = None
        r2g = r2_tg
        lrT = r2_tg
        for tq in range(NQT):
            ldq = dma(xvb_f, xvb_h[:, tq * 512:(tq + 1) * 512],
                      waits=(("dS", xvb_w[-1]), ("G", xvb_g[0])), sem="dS")
            p.wait("gpsimd", "dS", ldq)
            p.wait("gpsimd", "T", lrT)
            hv = p.ins("gpsimd", lambda e: e.tensor_copy(
                out=sm["xvb_hi"], in_=xvb_f), "G", 1)
            p.wait("gpsimd", "G", hv)
            gxv = p.ins("gpsimd", lambda e: e.tensor_tensor(
                sm["xvb_lo"], xvb_f, sm["xvb_hi"],
                ALU.subtract), "G", 1)
            xvb_g[0] = gxv
            p.wait("tensor", "G", gxv)
            fcb = []
            for jc in range(4):
                stf = pool.acquire()
                n_mm = [0]
                tf = None
                for xhl in (0, 1):
                    for dc in range(DC):
                        n_mm[0] += 1
                        fn = lambda e, b=stf["ap"], j=jc, d=dc, x_=xhl, q=tq, \
                            first=(n_mm[0] == 1): e.matmul(
                            b, H2[:, TFC_BASE + j * 1024 + d * 128:
                                  TFC_BASE + j * 1024 + (d + 1) * 128],
                            seg_rhs(X2T_HI if x_ == 0 else X2T_LO, q * 4, 4, d),
                            start=first, stop=False)
                        p.ins("tensor", fn)
                for pi, (uhl, xhl) in enumerate(((0, 0), (1, 0), (0, 1))):
                    uT = sm["uT_hi"] if uhl == 0 else sm["uT_lo"]
                    xb = sm["xvb_hi"] if xhl == 0 else sm["xvb_lo"]
                    fn = lambda e, b=stf["ap"], u_=uT, x_=xb, j=jc, \
                        last=(pi == 2): e.matmul(
                        b, u_[:, j * 128:(j + 1) * 128], x_,
                        start=False, stop=last)
                    if pi == 2:
                        tf = p.ins("tensor", fn, "T", 1)
                    else:
                        p.ins("tensor", fn)
                fcb.append((jc, stf, tf))
            lrT = fcb[-1][2]
            p.wait("vector", "T", fcb[-1][2])
            p.wait("vector", "S", p.val("S"))
            rbv = None
            for pr in (0, 1):
                if pr == 1:
                    p.wait("vector", "V", rbv)
                tsv = None
                for (jc, stf_, tf_) in fcb[pr * 2:pr * 2 + 2]:
                    c = jb * 4 + jc
                    tsv = p.ins("vector", lambda e, b=stf_["ap"], j_=jc, c_=c:
                                e.tensor_scalar(
                        A(26624 + (j_ % 2) * 512, 512), b,
                        sm["sfc2"][:, c_:c_ + 1], None, ALU.mult), "V", 1)
                p.wait("vector", "V", tsv)
                for (jc, stf_, tf_) in fcb[pr * 2:pr * 2 + 2]:
                    rbv = p.ins("vector", lambda e, b=stf_["ap"], j_=jc:
                                e.scalar_tensor_tensor(
                        f_ftmp(j_), A(26624 + (j_ % 2) * 512, 512), 0.0, b,
                        ALU.max, ALU.mult), "V", 1)
            for (jc, stf_, tf_) in fcb:
                pool.release(stf_, "V", rbv)
            if tq >= 1:
                emit_proj(tq - 1, hi_s, lo_v)
                r2g = p.val("T")
            p.wait("scalar", "V", rbv)
            p.wait("scalar", "T", r2g)
            nhi = None
            for (jc, stf_, tf_) in fcb:
                nhi = p.ins("scalar", lambda e, j_=jc: e.activation(
                    out=h2_r2hi(j_), in_=f_ftmp(j_), func=AF.Identity,
                    scale=1.0), "S", 1)
            hi_s = nhi
            p.wait("vector", "S", hi_s)
            p.wait("vector", "T", r2g)
            p.wait("vector", "V", rbv)
            for (jc, stf_, tf_) in fcb:
                lo_v = p.ins("vector", lambda e, j_=jc: e.tensor_tensor(
                    h2_r2lo(j_), f_ftmp(j_), h2_r2hi(j_), ALU.subtract), "V", 1)
        if jb + 1 < NJB:
            prefetch_fc(jb + 1, rbv)
        emit_proj(NQT - 1, hi_s, lo_v)
        r2_tg = p.val("T")

    ffn_v = p.val("V")
    p.wait("tensor", "V", ffn_v)
    p.wait("vector", "S", p.val("S"))
    p.wait("vector", "G", p.val("G"))
    p.wait("sync", "G", p.val("G"))
    store_vals = []
    xin_vals = []
    for tt in range(16):
        sbuf = tt % 2
        xw = [("dS", spill_v if tt == 0 else store_vals[tt - 1])]
        if tt >= 2:
            xw.append(("V", xin_vals[tt - 2]))
        xld = dma(f_xin(sbuf), xspill_h[:, tt * 1024:(tt + 1) * 1024], waits=tuple(xw))
        vv = None
        for eh in range(2):
            stx = pool.acquire()
            ti = None
            for k in range(4):
                ec = eh * 4 + k
                fn = lambda e, b=stx["ap"], t=tt, c=ec, o=k * 128: e.transpose(
                    b[:, o:o + 128], f_accffn(c)[:, t * 128:(t + 1) * 128], idnf)
                if k == 3:
                    ti = p.ins("tensor", fn, "T", 1)
                else:
                    p.ins("tensor", fn)
            p.wait("vector", "T", ti)
            p.wait("vector", "dL", xld)
            if tt >= 2 and eh == 0:
                p.wait("vector", "dS", store_vals[tt - 1])
            for k in range(4):
                ec = eh * 4 + k
                vv = p.ins("vector", lambda e, b=stx["ap"], s=sbuf, c=ec, o=k * 128:
                           e.tensor_add(
                    f_store(s)[:, c * 128:(c + 1) * 128], b[:, o:o + 128],
                    f_xin(s)[:, c * 128:(c + 1) * 128]),
                    "V", 1)
            pool.release(stx, "V", vv)
        xin_vals.append(vv)
        p.wait("sync", "V", vv)
        dv = p.ins("sync", lambda e, t=tt, s=sbuf: e.dma_start(
            out=out_h[t * 128:(t + 1) * 128, :], in_=f_store(s)), "dS", 16)
        store_vals.append(dv)
    p.wait("sync", "dS", p.val("dS"))
    p.wait("sync", "dG", p.val("dG"))

    with (
        nc.semaphore("s_dL") as s_dL,
        nc.semaphore("s_dS") as s_dS,
        nc.semaphore("s_T") as s_T,
        nc.semaphore("s_V") as s_V,
        nc.semaphore("s_S") as s_S,
        nc.semaphore("s_G") as s_G,
        nc.semaphore("s_dG") as s_dG,
        nc.Block() as block,
    ):
        p.sems = {"dL": s_dL, "dS": s_dS, "dG": s_dG, "T": s_T, "V": s_V,
                  "S": s_S, "G": s_G}

        @block.sync
        def _(sync):
            for fn in p.q["sync"]:
                fn(sync)

        @block.tensor
        def _(tensor):
            for fn in p.q["tensor"]:
                fn(tensor)

        @block.vector
        def _(vector):
            for fn in p.q["vector"]:
                fn(vector)

        @block.scalar
        def _(scalar):
            for fn in p.q["scalar"]:
                fn(scalar)

        @block.gpsimd
        def _(gpsimd):
            for fn in p.q["gpsimd"]:
                fn(gpsimd)

    return nc


_NC_CACHE = [None]


def kernel(L0_attn_x, L0_attn_Wq, L0_attn_bq, L0_attn_Wk, L0_attn_bk, L0_attn_Wv,
           L0_attn_bv, L0_attn_Wo, L0_attn_bo, L0_ffn_fast_b, L0_ffn_w_fc,
           L0_ffn_w_proj, L0_ffn_u, L0_ffn_v, L0_ffn_gate):
    x = np.ascontiguousarray(np.asarray(L0_attn_x, dtype=np.float32))
    Bn, Sn, Dn = x.shape
    assert (Bn, Sn, Dn) == (4, 4096, 1024)
    if _NC_CACHE[0] is None:
        _NC_CACHE[0] = build_nc()
    nc = _NC_CACHE[0]

    f32 = lambda a: np.ascontiguousarray(np.asarray(a, dtype=np.float32))
    com = {
        "wq": f32(L0_attn_Wq), "wk": f32(L0_attn_Wk), "wv": f32(L0_attn_Wv),
        "wo": f32(L0_attn_Wo),
        "bq": f32(L0_attn_bq).reshape(DC, 128), "bk": f32(L0_attn_bk).reshape(DC, 128),
        "bv": f32(L0_attn_bv).reshape(1, D), "bo": f32(L0_attn_bo).reshape(1, D),
        "wfc": f32(L0_ffn_w_fc), "wproj": f32(L0_ffn_w_proj),
        "u": f32(L0_ffn_u), "v": f32(L0_ffn_v), "fb": f32(L0_ffn_fast_b),
        "gate": f32(L0_ffn_gate).reshape(1, 1),
    }
    in_maps = []
    for c in range(8):
        b, h = c // 2, c % 2
        m = dict(com)
        m["x_own"] = np.ascontiguousarray(x[b, h * T:(h + 1) * T, :])
        m["x_oth"] = np.ascontiguousarray(x[b, (1 - h) * T:(2 - h) * T, :])
        in_maps.append(m)
    res = run_bass_kernel_spmd(nc, in_maps, core_ids=list(range(8)))
    out = np.empty((Bn, Sn, Dn), dtype=np.float32)
    for c in range(8):
        b, h = c // 2, c % 2
        out[b, h * T:(h + 1) * T, :] = np.asarray(res.results[c]["out"])
    return out
